# version 05
# baseline (speedup 1.0000x reference)
import numpy as np
import ml_dtypes
from contextlib import ExitStack

import concourse.bass as bass
import concourse.mybir as mybir
from concourse.bass_utils import run_bass_kernel_spmd

F32 = mybir.dt.float32
BF16 = mybir.dt.bfloat16
I32 = mybir.dt.int32
AF = mybir.ActivationFunctionType
ALU = mybir.AluOpType

B, S, H = 4, 4096, 2048
NTOK = B * S
DTR, NST = 64, 16
NE, FF = 8, 4096
NH, NKV, HD = 16, 4, 128
QPK = NH // NKV
OD = NKV * (QPK + 2) * HD
EPS = 1e-5
TOK = 2048
HH = 1024
NCORES = 8
MAGIC = 0x5CA1AB1E
NEG = np.float32(-1.0e9)
ATT_SCALE = float(np.float32(1.0) / np.sqrt(np.float32(HD)))

DEBUG = False


def build_nc(debug=DEBUG):
    nc = bass.Bass("TRN2", target_bir_lowering=False)

    def din(name, shape, dt=F32):
        return nc.dram_tensor(name, shape, dt, kind="ExternalInput").ap()

    x_own = din("x_own", [TOK, H])
    x_bb = din("x_bb", [S, H])
    x_ext = din("x_ext", [3 * TOK, H])
    wdtin_p = din("wdtin_p", [DTR, H])
    wbc_p = din("wbc_p", [2 * NST, H])
    wdtout_h = din("wdtout_h", [HH, DTR])
    bdt_h = din("bdt_h", [HH])
    alog_h = din("alog_h", [HH, NST])
    d_h = din("d_h", [HH])
    ln1_w = din("ln1_w", [H])
    ln1_b = din("ln1_b", [H])
    ln2_w = din("ln2_w", [H])
    ln2_b = din("ln2_b", [H])
    qkv_w = din("qkv_w", [OD, H])
    proj_w = din("proj_w", [H, H])
    router_w = din("router_w", [H, NE])
    moe_w1 = din("moe_w1", [NE, H, FF])
    moe_w2 = din("moe_w2", [NE, FF, H])
    rank_in = din("rank", [1, 1], I32)
    eye_bf = din("eye_bf", [128, 128], BF16)
    ones_row = din("ones_row", [1, 128], BF16)
    sel_bc = din("sel_bc", [32, 32 * 128], BF16)
    mv_t = din("mv_t", [128, 64])
    ma_t = din("ma_t", [128, 64])

    out_slab = nc.dram_tensor("out_slab", [TOK, H], F32, kind="ExternalOutput").ap()
    dbg = {}
    if debug:
        def dout(name, shape, dt=F32):
            dbg[name] = nc.dram_tensor(name, shape, dt, kind="ExternalOutput").ap()
        dout("dbg_xn_own", [TOK, H])
        dout("dbg_dt", [HH, S])
        dout("dbg_bct", [32, S])
        dout("dbg_ytok", [S, HH])
        dout("dbg_attn", [TOK, H])
        dout("dbg_ln1", [TOK, H])
        dout("dbg_qkvT", [OD, TOK])
        dout("dbg_kvT", [3 * 1024, TOK])
        dout("dbg_probs", [128, 64])
        dout("dbg_dtpre", [DTR, S])
        dout("dbg_wdi", [128, 16 * DTR])
        dout("dbg_wdi_t1", [128, 16 * DTR])
        dout("dbg_wdi_t2", [128, 16 * DTR])
        dout("dbg_haa", [TOK, H])
        dout("dbg_xln2", [TOK, H])
        dout("dbg_gates", [128, 128])
        dout("dbg_moe", [TOK, H])

    def scr(name, shape, dt):
        return nc.dram_tensor(name, shape, dt).ap()

    xn_bb = scr("xn_bb", [S, H], BF16)
    xnT_bb = scr("xnT_bb", [H, S], BF16)
    xn_own_d = scr("xn_own_d", [TOK, H], F32)
    ln1_own_d = scr("ln1_own_d", [TOK, H], BF16)
    ln1_ext_d = scr("ln1_ext_d", [3 * TOK, H], BF16)
    qkvw_bf = scr("qkvw_bf", [OD, H], BF16)
    qkvwT_d = scr("qkvwT_d", [H, OD], BF16)
    projw_bf = scr("projw_bf", [H, H], BF16)
    projwT_d = scr("projwT_d", [H, H], BF16)
    dt_d = scr("dt_d", [HH, S], BF16)
    du_d = scr("du_d", [HH, S], BF16)
    y_d = scr("y_d", [HH, S], F32)
    y_bf_d = scr("y_bf_d", [HH, S], BF16)
    y_tok_d = scr("y_tok_d", [S, HH], BF16)
    y_par_d = scr("y_par_d", [TOK, HH], BF16)
    qkvT_d = scr("qkvT_d", [OD, TOK], BF16)
    kvT_d = scr("kvT_d", [3 * 1024, TOK], BF16)
    attn_d = scr("attn_d", [TOK, H], BF16)
    haa_d = scr("haa_d", [TOK, H], F32)
    xln2_d = scr("xln2_d", [TOK, H], BF16)
    moeacc = scr("moeacc", [TOK, H], F32)
    bct_d = scr("bct_d", [32, S], BF16)
    y_cat_d = scr("y_cat_d", [TOK, H], BF16)

    xchg_y = nc.dram_tensor("xchg_y", [2, TOK, HH], BF16, addr_space="Shared").ap()
    xchg_fl = nc.dram_tensor("xchg_fl", [2, 16], I32, addr_space="Shared").ap()

    with ExitStack() as ctx:
        def sbt(name, p, f, dt):
            return ctx.enter_context(nc.sbuf_tensor(name, [p, f], dt))

        def sem(name):
            return ctx.enter_context(nc.semaphore(name))

        arA = sbt("arA", 128, 32768, BF16)
        arB = sbt("arB", 128, 32768, BF16)
        arC = sbt("arC", 128, 16384, BF16)
        arD = sbt("arD", 128, 4096, F32)
        arE = sbt("arE", 128, 4096, F32)
        arF = sbt("arF", 128, 4096, BF16)
        bn6 = sbt("bn6", 128, 24, F32)
        bn2 = sbt("bn2", 128, 2, F32)
        gbg = arA[:, 0:2048]
        gbb = arA[:, 2048:4096]
        stats = arA[:, 4096:5056].bitcast(F32)
        onesr_sb = sbt("onesr_sb", 1, 128, BF16)
        mv_sb = sbt("mv_sb", 128, 64, F32)
        ma_sb = sbt("ma_sb", 128, 64, F32)
        a_sb = sbt("a_sb", 128, 8 * NST, F32)
        d_sb = sbt("d_sb", 128, 8, F32)
        bdt_sb = sbt("bdt_sb", 128, 8, F32)
        routw_sb = sbt("routw_sb", 128, 16 * NE, BF16)
        cpool = sbt("cpool", 128, 4, F32)
        wdtinT_ded = sbt("wdtinT_ded", 128, 16 * DTR, BF16)
        gates_sb = sbt("gates_sb", 128, 16 * NE, F32)
        wdtinT_sb = wdtinT_ded[:, :]
        wbcT_sb = arC[:, 1024:1024 + 16 * 32]
        wdtoutT_sb = arC[0:DTR, 1536:1536 + HH]
        wdi_sb = arC[0:DTR, 2560:2560 + H]
        wbc_sb = arC[0:32, 4608:4608 + H]
        wdo_sb = arC[:, 6656:6656 + 8 * DTR]
        eye_sb = arC[:, 7168:7296]
        dtpre_bf = arC[0:DTR, 8192:8192 + S]
        sel_re = arC[0:32, 0:4096]
        bct_re = arC[0:32, 4096:4096 + S]
        TG = 1024
        NTG = S // TG
        mdt = arB[:, 0:8 * TG]
        mdu = arB[:, 8 * TG:16 * TG]
        mbc = arB[:, 16384:16384 + 2 * TG]
        ma_ = arB[:, 18432:18432 + 2 * TG]
        mbu = arB[:, 20480:20480 + 2 * TG]
        mx = arB[:, 22528:22528 + 2 * TG]
        mcx = arB[:, 24576:24576 + 2 * TG]
        mst = arB[:, 26624:26880].bitcast(F32)
        sqa = arB[:, 0:3072]
        sqb = arB[:, 3072:6144]
        sso = arB[:, 6144:8192]
        satt = arB[:, 8192:8320]
        ssc = arB[:, 8320:8448].bitcast(F32)
        ssm = arB[:, 8448:8544].bitcast(F32)
        ptmp = arB[:, 8576:8704]
        pb = [ctx.enter_context(nc.psum_tensor(f"pb{i}", [128, 512], F32)) for i in range(7)]
        pbf7 = ctx.enter_context(nc.psum_tensor("pbf7", [128, 1024], BF16))

        sm_sync = sem("sm_sync")
        sm_gp = sem("sm_gp")
        sm_sc = sem("sm_sc")
        sm_v = sem("sm_v")
        sm_s = sem("sm_s")
        sm_p = sem("sm_p")
        sm_t = sem("sm_t")
        sm_bar = sem("sm_bar")

        cnt = {"sync": 0, "gp": 0, "sc": 0, "v": 0, "s": 0, "p": 0, "t": 0, "bar": 0}
        rec = {"sync": [], "gpsimd": [], "vector": [], "scalar": [], "tensor": []}

        def on(engine, fn):
            rec[engine].append(fn)

        QSEM = {"sync": sm_sync, "gpsimd": sm_gp, "scalar": sm_sc}
        QCNT = {"sync": "sync", "gpsimd": "gp", "scalar": "sc"}
        CSEM = {"v": sm_v, "s": sm_s, "t": sm_t, "p": sm_p,
                "sync": sm_sync, "gp": sm_gp, "sc": sm_sc}
        ENG_OF = {"v": "vector", "s": "scalar", "p": "gpsimd", "t": "tensor"}

        def dma(queue, out, in_, **kw):
            cnt[QCNT[queue]] += 1
            n = cnt[QCNT[queue]]
            s = QSEM[queue]
            def fn(eng, out=out, in_=in_, kw=dict(kw), s=s):
                eng.dma_start(out=out, in_=in_, **kw).then_inc(s, 16)
            on(queue, fn)
            return n

        def dma_t(queue, out, in_):
            cnt[QCNT[queue]] += 1
            n = cnt[QCNT[queue]]
            s = QSEM[queue]
            def fn(eng, out=out, in_=in_, s=s):
                eng.dma_start_transpose(out=out, in_=in_).then_inc(s, 16)
            on(queue, fn)
            return n

        def wt(engine, chain, n):
            if n <= 0:
                return
            s = CSEM[chain]
            mult = 16 if chain in ("sync", "gp", "sc") else 1
            on(engine, lambda eng, s=s, n=n, m=mult: eng.wait_ge(s, m * n))

        def vop(fn):
            cnt["v"] += 1
            n = cnt["v"]
            def run(eng, fn=fn, n=n):
                if n > 1:
                    eng.wait_ge(sm_v, n - 1)
                fn(eng).then_inc(sm_v, 1)
            on("vector", run)
            return n

        def sop(fn):
            cnt["s"] += 1
            n = cnt["s"]
            def run(eng, fn=fn, n=n):
                if n > 1:
                    eng.wait_ge(sm_s, n - 1)
                fn(eng).then_inc(sm_s, 1)
            on("scalar", run)
            return n

        def pop_(fn):
            cnt["p"] += 1
            n = cnt["p"]
            def run(eng, fn=fn, n=n):
                if n > 1:
                    eng.wait_ge(sm_p, n - 1)
                fn(eng).then_inc(sm_p, 1)
            on("gpsimd", run)
            return n

        def top(fn):
            cnt["t"] += 1
            def run(eng, fn=fn):
                fn(eng).then_inc(sm_t, 1)
            on("tensor", run)
            return cnt["t"]

        def top_q(fn):
            on("tensor", lambda eng, fn=fn: fn(eng))

        def barrier():
            cnt["bar"] += 1
            tgt = 5 * cnt["bar"]
            d1, d2, d3 = cnt["sync"], cnt["gp"], cnt["sc"]
            for engname in ("sync", "gpsimd", "vector", "scalar", "tensor"):
                def fn(eng, tgt=tgt, d1=d1, d2=d2, d3=d3):
                    if d1:
                        eng.wait_ge(sm_sync, 16 * d1)
                    if d2:
                        eng.wait_ge(sm_gp, 16 * d2)
                    if d3:
                        eng.wait_ge(sm_sc, 16 * d3)
                    eng.nop().then_inc(sm_bar, 1)
                    eng.wait_ge(sm_bar, tgt)
                on(engname, fn)

        bank_tok = [("v", 0)] * 8

        def mm_group(bank, mms, evac, m=128, nn=512):
            ch, n = bank_tok[bank]
            wt("tensor", ch, n)
            for i, (lf, rf) in enumerate(mms):
                st = (i == 0)
                sp = (i == len(mms) - 1)
                if sp:
                    t_tok = top(lambda eng, lf=lf, rf=rf, st=st, sp=sp:
                                eng.matmul(pb[bank][0:m, 0:nn], lf(), rf(), start=st, stop=sp))
                else:
                    top_q(lambda eng, lf=lf, rf=rf, st=st, sp=sp:
                          eng.matmul(pb[bank][0:m, 0:nn], lf(), rf(), start=st, stop=sp))
            ch2, fn2 = evac
            wt(ENG_OF[ch2], "t", t_tok)
            if ch2 == "v":
                etok = vop(lambda eng, fn2=fn2: fn2(eng, pb[bank]))
            elif ch2 == "s":
                etok = sop(lambda eng, fn2=fn2: fn2(eng, pb[bank]))
            bank_tok[bank] = (ch2, etok)
            return etok, t_tok

        rk_cell = {}

        def get_rank(engine):
            if engine not in rk_cell:
                cell = {}
                rk_cell[engine] = cell
                def fn(eng, cell=cell):
                    cell["rk"] = eng.value_load(rank_in)
                    cell["k"] = cell["rk"] % 2
                on(engine, fn)
            return rk_cell[engine]

        nc.const_aps.register(F32, EPS, cpool[:, 0:1])
        def fn_cpool(eng):
            eng.memset(cpool[:, 0:1], EPS).then_inc(sm_v, 1)
        on("vector", fn_cpool)
        cnt["v"] += 1
        v_cpool = cnt["v"]
        wt("scalar", "v", v_cpool)
        dma("sync", eye_sb[:], eye_bf)
        dma("sync", onesr_sb[:], ones_row)
        dma("sync", mv_sb[:], mv_t)
        dma("sync", ma_sb[:], ma_t)
        for hc in range(8):
            dma("sync", a_sb[:, hc * NST:(hc + 1) * NST], alog_h[hc * 128:(hc + 1) * 128, :])
            dma("sync", d_sb[:, hc:hc + 1], d_h[hc * 128:(hc + 1) * 128][:, None])
            dma("sync", bdt_sb[:, hc:hc + 1], bdt_h[hc * 128:(hc + 1) * 128][:, None])
        n_al = cnt["sync"]
        gamma_last = {"t": 0, "s": 0}
        def emit_gamma(w_vec, b_vec):
            r1 = dma("sync", arD[0:1, 0:H], w_vec[None, :])
            r2 = dma("sync", arD[0:1, 2048:2048 + H], b_vec[None, :])
            wt("vector", "sync", r2)
            vg = vop(lambda eng: eng.tensor_copy(arF[0:1, 0:H], arD[0:1, 0:H]))
            vg = vop(lambda eng: eng.tensor_copy(arF[0:1, H:2 * H], arD[0:1, 2048:2048 + H]))
            wt("tensor", "v", vg)
            wt("tensor", "sync", 2)
            for gi, dst in enumerate([gbg, gbb]):
                for col in range(4):
                    ch, n = bank_tok[6]
                    wt("tensor", ch, n)
                    tk = top(lambda eng, gi=gi, col=col: eng.matmul(
                        pb[6][:], onesr_sb[:],
                        arF[0:1, gi * H + col * 512: gi * H + (col + 1) * 512],
                        start=True, stop=True))
                    wt("scalar", "t", tk)
                    bank_tok[6] = ("s", sop(lambda eng, dst=dst, col=col: eng.copy(
                        dst[:, col * 512:(col + 1) * 512], pb[6][:])))
            gamma_last["t"] = cnt["t"]
            gamma_last["s"] = cnt["s"]
        emit_gamma(ln1_w, ln1_b)
        wt("scalar", "sync", n_al)
        s_a1 = sop(lambda eng: eng.activation(a_sb[:], a_sb[:], AF.Exp))
        wt("vector", "s", s_a1)
        vop(lambda eng: eng.tensor_scalar(a_sb[:], a_sb[:], -1.0, None, ALU.mult))
        for hc in range(16):
            dma("gpsimd", routw_sb[:, hc * NE:(hc + 1) * NE], router_w[hc * 128:(hc + 1) * 128, :])

        def cast_rt(src, dst, rows, cols, tag):
            toks = []
            nblk = rows // 128
            for i in range(nblk):
                bi = i % 2
                bslc = arF[:, bi * cols:(bi + 1) * cols]
                if i >= 2:
                    wt("gpsimd", "gp", toks[i - 2][1])
                t_in = dma("gpsimd", bslc, src[i * 128:(i + 1) * 128, :])
                wt("gpsimd", "gp", t_in)
                t_out = dma("gpsimd", dst[i * 128:(i + 1) * 128, :], bslc)
                toks.append((t_in, t_out))
            return toks[-1][1]

        wt("gpsimd", "t", gamma_last["t"])
        qkv_cast = cast_rt(qkv_w, qkvw_bf, OD, H, "qkv")
        proj_cast = cast_rt(proj_w, projw_bf, H, H, "proj")
        wt("sync", "gp", qkv_cast)
        tq = []
        for hc in range(16):
            bi = hc % 2
            tbb = arB[:, bi * OD:(bi + 1) * OD]
            if hc >= 2:
                wt("sync", "sync", tq[hc - 2])
            t1 = dma_t("sync", tbb, qkvw_bf[:, hc * 128:(hc + 1) * 128])
            wt("sync", "sync", t1)
            tq.append(dma("sync", qkvwT_d[hc * 128:(hc + 1) * 128, :], tbb))
        wt("sync", "gp", proj_cast)
        tp = []
        for hc in range(16):
            bi = hc % 2
            tbb = arB[:, bi * OD:bi * OD + H]
            if hc >= 2:
                wt("sync", "sync", tp[hc - 2])
            t1 = dma_t("sync", tbb, projw_bf[:, hc * 128:(hc + 1) * 128])
            wt("sync", "sync", t1)
            tp.append(dma("sync", projwT_d[hc * 128:(hc + 1) * 128, :], tbb))
        n_wdi = dma("gpsimd", wdi_sb[:], wdtin_p)
        n_wbc = dma("gpsimd", wbc_sb[:], wbc_p)
        for i in range(8):
            dma("gpsimd", wdo_sb[:, i * DTR:(i + 1) * DTR], wdtout_h[i * 128:(i + 1) * 128, :])
        n_wdo = cnt["gp"]
        pbf = pbf7[:]
        wt("tensor", "gp", n_wdo)
        wt("tensor", "sync", 1)
        for hc in range(16):
            ch, n = bank_tok[7]
            wt("tensor", ch, n)
            tk = top(lambda eng, hc=hc: eng.transpose(
                pbf[:, 0:DTR], wdi_sb[0:DTR, hc * 128:(hc + 1) * 128], eye_sb[0:DTR, 0:DTR]))
            wt("scalar", "t", tk)
            bank_tok[7] = ("s", sop(lambda eng, hc=hc: eng.copy(
                wdtinT_sb[:, hc * DTR:(hc + 1) * DTR], pbf[:, 0:DTR])))
        wt("tensor", "gp", n_wbc)
        for hc in range(16):
            ch, n = bank_tok[7]
            wt("tensor", ch, n)
            tk = top(lambda eng, hc=hc: eng.transpose(
                pbf[:, 0:32], wbc_sb[:, hc * 128:(hc + 1) * 128], eye_sb[0:32, 0:32]))
            wt("scalar", "t", tk)
            bank_tok[7] = ("s", sop(lambda eng, hc=hc: eng.copy(
                wbcT_sb[:, hc * 32:(hc + 1) * 32], pbf[:, 0:32])))
        wt("tensor", "gp", n_wdo)
        for i in range(8):
            ch, n = bank_tok[7]
            wt("tensor", ch, n)
            tk = top(lambda eng, i=i: eng.transpose(
                pbf[0:DTR, 0:128], wdo_sb[:, i * DTR:(i + 1) * DTR], eye_sb[:]))
            wt("scalar", "t", tk)
            bank_tok[7] = ("s", sop(lambda eng, i=i: eng.copy(
                wdtoutT_sb[0:DTR, i * 128:(i + 1) * 128], pbf[0:DTR, 0:128])))
        barrier()
        if debug:
            vt1 = vop(lambda eng: eng.tensor_copy(arD[:, 0:1024], wdtinT_sb[:]))
            wt("sync", "v", vt1)
            dma("sync", dbg["dbg_wdi_t1"], arD[:, 0:1024])
            barrier()

        xsrc = [(x_bb, 0, 32, "bb"), (x_own, 32, 16, "own"), (x_ext, 48, 48, "ext")]
        NMU, NRS, NR1, NSC = 0, 96, 192, 384
        xbuf = [arD[:, 0:2048], arD[:, 2048:4096]]
        tile_jobs = []
        for src, colbase, ntile, tag in xsrc:
            for i in range(ntile):
                tile_jobs.append((src, i, colbase + i, tag))
        lastv = {0: 0, 1: 0}
        for j, (src, i, col, tag) in enumerate(tile_jobs):
            bi = j % 2
            wt("sync", "v", lastv[bi])
            ld = dma("sync", xbuf[bi], src[i * 128:(i + 1) * 128, :])
            wt("vector", "sync", ld)
            for q in range(4):
                v1 = vop(lambda eng, bi=bi, q=q: eng.bn_stats(bn6[:, q * 6:(q + 1) * 6], xbuf[bi][:, q * 512:(q + 1) * 512]))
            vop(lambda eng: eng.bn_aggr(bn2[:], bn6[:]))
            vop(lambda eng, col=col: eng.tensor_copy(stats[:, NMU + col:NMU + col + 1], bn2[:, 0:1]))
            vop(lambda eng, col=col: eng.tensor_copy(stats[:, NRS + col:NRS + col + 1], bn2[:, 1:2]))
            lastv[bi] = v1
        vop(lambda eng: eng.tensor_scalar(stats[:, NMU:NMU + 96], stats[:, NMU:NMU + 96], -1.0, None, ALU.mult))
        v_pre = cnt["v"]
        wt("scalar", "v", v_pre)
        s1 = sop(lambda eng: eng.activation(stats[:, NSC:NSC + 96], stats[:, NRS:NRS + 96], AF.Sqrt, bias=EPS))
        wt("vector", "s", s1)
        vop(lambda eng: eng.reciprocal(stats[:, NSC:NSC + 96], stats[:, NSC:NSC + 96]))
        vop(lambda eng: eng.tensor_tensor(stats[:, NR1:NR1 + 96], stats[:, NRS:NRS + 96], stats[:, NSC:NSC + 96], ALU.mult))
        v2 = vop(lambda eng: eng.tensor_tensor(stats[:, NR1:NR1 + 96], stats[:, NR1:NR1 + 96], stats[:, NSC:NSC + 96], ALU.mult))
        wt("scalar", "v", v2)
        s2 = sop(lambda eng: eng.activation(stats[:, NR1:NR1 + 96], stats[:, NR1:NR1 + 96], AF.Sqrt, bias=EPS))
        wt("vector", "s", s2)
        vop(lambda eng: eng.reciprocal(stats[:, NR1:NR1 + 96], stats[:, NR1:NR1 + 96]))
        v_stats = vop(lambda eng: eng.tensor_copy(stats[:, NRS:NRS + 96], stats[:, NSC:NSC + 96]))
        xnbuf = [arB[:, 0:2048], arB[:, 2048:4096]]
        lnbuf = [arB[:, 4096:6144], arB[:, 6144:8192]]
        lastv = {0: 0, 1: 0}
        lastwr = {0: 0, 1: 0}
        last_e = 0
        for j, (src, i, col, tag) in enumerate(tile_jobs):
            bi = j % 2
            wt("sync", "v", lastv[bi])
            ld = dma("sync", xbuf[bi], src[i * 128:(i + 1) * 128, :])
            wt("vector", "sync", ld)
            wt("vector", "v", v_stats)
            wt("vector", "sync", lastwr[bi])
            v_xn = vop(lambda eng, bi=bi, col=col: eng.tensor_scalar(
                xnbuf[bi], xbuf[bi], stats[:, NMU + col:NMU + col + 1],
                stats[:, NRS + col:NRS + col + 1], ALU.add, ALU.mult))
            if tag == "bb":
                wt("sync", "v", v_xn)
                lastwr[bi] = dma("sync", xn_bb[i * 128:(i + 1) * 128, :], xnbuf[bi])
                lastv[bi] = v_xn
            else:
                wt("vector", "sync", last_e)
                v_f = vop(lambda eng, bi=bi, col=col: eng.tensor_scalar(
                    arE[:, 0:2048], xbuf[bi], stats[:, NMU + col:NMU + col + 1],
                    stats[:, NRS + col:NRS + col + 1], ALU.add, ALU.mult))
                lastv[bi] = v_f
                if tag == "own":
                    wt("sync", "v", v_f)
                    last_e = dma("sync", xn_own_d[i * 128:(i + 1) * 128, :], arE[:, 0:2048])
                v_l1 = vop(lambda eng, col=col: eng.scalar_tensor_tensor(
                    arE[:, 2048:4096], arE[:, 0:2048], stats[:, NR1 + col:NR1 + col + 1],
                    gbg[:], ALU.mult, ALU.mult))
                v_l2 = vop(lambda eng, bi=bi: eng.tensor_tensor(
                    lnbuf[bi], arE[:, 2048:4096], gbb[:], ALU.add))
                wt("sync", "v", v_l2)
                if tag == "own":
                    lastwr[bi] = dma("sync", ln1_own_d[i * 128:(i + 1) * 128, :], lnbuf[bi])
                else:
                    lastwr[bi] = dma("sync", ln1_ext_d[i * 128:(i + 1) * 128, :], lnbuf[bi])
        barrier()
        if debug:
            for i in range(16):
                ld = dma("sync", arD[:, 0:2048], xn_own_d[i * 128:(i + 1) * 128, :])
                wt("sync", "sync", ld)
                dma("sync", dbg["dbg_xn_own"][i * 128:(i + 1) * 128, :], arD[:, 0:2048])
            barrier()

        if debug:
            vt2 = vop(lambda eng: eng.tensor_copy(arD[:, 0:1024], wdtinT_sb[:]))
            wt("sync", "v", vt2)
            dma("sync", dbg["dbg_wdi_t2"], arD[:, 0:1024])
            barrier()
        tprev = {0: 0, 1: 0}
        for hc in range(16):
            bi = hc % 2
            wt("sync", "sync", tprev[bi])
            t1 = dma_t("sync", arB[:, bi * S:(bi + 1) * S], xn_bb[:, hc * 128:(hc + 1) * 128])
            wt("sync", "sync", t1)
            tprev[bi] = dma("sync", xnT_bb[hc * 128:(hc + 1) * 128, :], arB[:, bi * S:(bi + 1) * S])
        barrier()

        bct_stage_wr = 0
        for tc in range(8):
            lds = []
            for hc in range(16):
                lds.append(dma("sync", arB[:, hc * 512:(hc + 1) * 512],
                               xnT_bb[hc * 128:(hc + 1) * 128, tc * 512:(tc + 1) * 512]))
            wt("tensor", "sync", lds[-1])
            mms = [(lambda hc=hc: wdtinT_sb[:, hc * DTR:(hc + 1) * DTR],
                    lambda hc=hc: arB[:, hc * 512:(hc + 1) * 512]) for hc in range(16)]
            def ev_dt(eng, ps, tc=tc):
                return eng.copy(dtpre_bf[0:DTR, tc * 512:(tc + 1) * 512], ps[0:DTR, :])
            last_dtpre_ev, _ = mm_group(0, mms, ("s", ev_dt), m=DTR)
            mmsb = [(lambda hc=hc: wbcT_sb[:, hc * 32:(hc + 1) * 32],
                     lambda hc=hc: arB[:, hc * 512:(hc + 1) * 512]) for hc in range(16)]
            wt("scalar", "sync", bct_stage_wr)
            def ev_bc(eng, ps, tc=tc):
                return eng.copy(arF[0:32, 1536:2048], ps[0:32, :])
            etok, _ = mm_group(1, mmsb, ("s", ev_bc), m=32)
            wt("sync", "s", etok)
            bct_stage_wr = dma("sync", bct_d[0:32, tc * 512:(tc + 1) * 512], arF[0:32, 1536:2048])
        barrier()
        if debug:
            v9 = vop(lambda eng: eng.tensor_copy(arD[0:DTR, 0:4096].bitcast(BF16)[:, 0:4096], dtpre_bf[:]))
            v9b = vop(lambda eng: eng.tensor_copy(arE[0:DTR, 0:4096], arD[0:DTR, 0:4096].bitcast(BF16)[:, 0:4096]))
            wt("sync", "v", v9b)
            dma("sync", dbg["dbg_dtpre"], arE[0:DTR, 0:4096])
            v9c = vop(lambda eng: eng.tensor_copy(arD[:, 0:1024], wdtinT_sb[:]))
            wt("sync", "v", v9c)
            dma("sync", dbg["dbg_wdi"], arD[:, 0:1024])
            barrier()
        if debug:
            ldb = dma("sync", arF[0:32, 0:4096], bct_d[:, :])
            wt("vector", "sync", ldb)
            v1 = vop(lambda eng: eng.tensor_copy(arE[0:32, 0:S], arF[0:32, 0:4096]))
            wt("sync", "v", v1)
            dma("sync", dbg["dbg_bct"], arE[0:32, 0:S])
            barrier()

        sp1 = arE[:, 0:512]
        sp2 = arE[:, 512:1024]
        dtf = arE[:, 1024:1536]
        dterm = arE[:, 1536:2048]
        u_t = arF[:, 0:512]
        dt_bf_t = arF[:, 512:1024]
        du_t = arF[:, 1024:1536]
        prev_wr = [0, 0, 0]
        prev_u = 0
        prev_vdt = 0
        wt("tensor", "s", last_dtpre_ev)
        for hc in range(8):
            for tc in range(8):
                ch, n = bank_tok[2]
                wt("tensor", ch, n)
                tk = top(lambda eng, hc=hc, tc=tc: eng.matmul(
                    pb[2][:], wdtoutT_sb[0:DTR, hc * 128:(hc + 1) * 128],
                    dtpre_bf[0:DTR, tc * 512:(tc + 1) * 512], start=True, stop=True))
                wt("scalar", "v", prev_vdt)
                wt("scalar", "t", tk)
                sop(lambda eng, hc=hc: eng.activation(sp1, pb[2][:], AF.Relu, bias=bdt_sb[:, hc:hc + 1]))
                s_abs = sop(lambda eng, hc=hc: eng.activation(sp2, pb[2][:], AF.Abs, bias=bdt_sb[:, hc:hc + 1]))
                bank_tok[2] = ("s", s_abs)
                sop(lambda eng: eng.activation(sp2, sp2, AF.Exp, scale=-1.0))
                s_sp = sop(lambda eng: eng.activation(sp2, sp2, AF.Ln, bias=1.0))
                wt("vector", "s", s_sp)
                v_dt = vop(lambda eng: eng.tensor_tensor(dtf, sp1, sp2, ALU.add))
                prev_vdt = v_dt
                wt("vector", "sync", prev_wr[0])
                v_dtb = vop(lambda eng: eng.tensor_copy(dt_bf_t, dtf))
                wt("sync", "v", v_dtb)
                prev_wr[0] = dma("sync", dt_d[hc * 128:(hc + 1) * 128, tc * 512:(tc + 1) * 512], dt_bf_t)
                wt("sync", "v", prev_u)
                lu = dma("sync", u_t, xnT_bb[hc * 128:(hc + 1) * 128, tc * 512:(tc + 1) * 512])
                wt("vector", "sync", lu)
                wt("vector", "sync", prev_wr[1])
                v_du = vop(lambda eng: eng.scalar_tensor_tensor(du_t, u_t, 1.0, dt_bf_t, ALU.mult, ALU.mult))
                wt("sync", "v", v_du)
                prev_wr[1] = dma("sync", du_d[hc * 128:(hc + 1) * 128, tc * 512:(tc + 1) * 512], du_t)
                wt("vector", "sync", prev_wr[2])
                prev_u = vop(lambda eng, hc=hc: eng.tensor_scalar(
                    dterm, u_t, d_sb[:, hc:hc + 1], None, ALU.mult))
                wt("sync", "v", prev_u)
                prev_wr[2] = dma("sync", y_d[hc * 128:(hc + 1) * 128, tc * 512:(tc + 1) * 512], dterm)
        barrier()
        if debug:
            for hc in range(8):
                ld = dma("sync", arF[:, 0:4096], dt_d[hc * 128:(hc + 1) * 128, :])
                wt("vector", "sync", ld)
                v1 = vop(lambda eng: eng.tensor_copy(arD[:, 0:4096], arF[:, 0:4096]))
                wt("sync", "v", v1)
                dma("sync", dbg["dbg_dt"][hc * 128:(hc + 1) * 128, :], arD[:, 0:4096])
            barrier()

        for hc in range(16):
            dma_t("sync", arA[:, hc * TOK:(hc + 1) * TOK], ln1_own_d[:, hc * 128:(hc + 1) * 128])
        barrier()
        qstage = [arF[:, i * 512:(i + 1) * 512] for i in range(4)]
        qstage_wr = [0, 0, 0, 0]
        for half in range(2):
            lds = []
            for hc in range(16):
                lds.append(dma("sync", arB[:, hc * 1536:(hc + 1) * 1536],
                               qkvwT_d[hc * 128:(hc + 1) * 128, half * 1536:(half + 1) * 1536]))
            wt("tensor", "sync", lds[-1])
            for oc in range(12):
                for tcol in range(4):
                    mms = [(lambda hc=hc, oc=oc: arB[:, hc * 1536 + oc * 128: hc * 1536 + (oc + 1) * 128],
                            lambda hc=hc, tcol=tcol: arA[:, hc * TOK + tcol * 512: hc * TOK + (tcol + 1) * 512])
                           for hc in range(16)]
                    si = tcol
                    wt("scalar", "sync", qstage_wr[si])
                    def ev(eng, ps, si=si):
                        return eng.copy(qstage[si], ps[:])
                    etok, _ = mm_group(tcol % 2, mms, ("s", ev))
                    wt("sync", "s", etok)
                    qstage_wr[si] = dma("sync",
                                        qkvT_d[(half * 12 + oc) * 128:(half * 12 + oc + 1) * 128,
                                               tcol * 512:(tcol + 1) * 512], qstage[si])
        barrier()
        kv_cols = [(g * 768 + 512, g * 768 + 768) for g in range(NKV)]
        last_kv_mm = 0
        for slot in range(3):
            for hc in range(16):
                dma_t("sync", arB[:, hc * TOK:(hc + 1) * TOK],
                      ln1_ext_d[slot * TOK:(slot + 1) * TOK, hc * 128:(hc + 1) * 128])
            ln_t = cnt["sync"]
            wt("scalar", "t", last_kv_mm)
            lds = []
            for hc in range(16):
                for g in range(NKV):
                    c0, c1 = kv_cols[g]
                    lds.append(dma("scalar", arC[:, hc * 1024 + g * 256: hc * 1024 + (g + 1) * 256],
                                   qkvwT_d[hc * 128:(hc + 1) * 128, c0:c1]))
            wt("tensor", "sync", ln_t)
            wt("tensor", "sc", lds[-1])
            for oc in range(8):
                for tcol in range(4):
                    mms = [(lambda hc=hc, oc=oc: arC[:, hc * 1024 + oc * 128: hc * 1024 + (oc + 1) * 128],
                            lambda hc=hc, tcol=tcol: arB[:, hc * TOK + tcol * 512: hc * TOK + (tcol + 1) * 512])
                           for hc in range(16)]
                    si = tcol
                    wt("scalar", "sync", qstage_wr[si])
                    def ev(eng, ps, si=si):
                        return eng.copy(qstage[si], ps[:])
                    etok, mmtok = mm_group(tcol % 2, mms, ("s", ev))
                    last_kv_mm = mmtok
                    wt("sync", "s", etok)
                    qstage_wr[si] = dma("sync",
                                        kvT_d[(slot * 8 + oc) * 128:(slot * 8 + oc + 1) * 128,
                                              tcol * 512:(tcol + 1) * 512], qstage[si])
        barrier()

        if debug:
            for i in range(16):
                ld = dma("sync", arF[:, 0:2048], ln1_own_d[i * 128:(i + 1) * 128, :])
                wt("vector", "sync", ld)
                v1 = vop(lambda eng: eng.tensor_copy(arD[:, 0:2048], arF[:, 0:2048]))
                wt("sync", "v", v1)
                dma("sync", dbg["dbg_ln1"][i * 128:(i + 1) * 128, :], arD[:, 0:2048])
            for i in range(24):
                ld = dma("sync", arF[:, 0:2048], qkvT_d[i * 128:(i + 1) * 128, :])
                wt("vector", "sync", ld)
                v1 = vop(lambda eng: eng.tensor_copy(arD[:, 0:2048], arF[:, 0:2048]))
                wt("sync", "v", v1)
                dma("sync", dbg["dbg_qkvT"][i * 128:(i + 1) * 128, :], arD[:, 0:2048])
            for i in range(24):
                ld = dma("sync", arF[:, 0:2048], kvT_d[i * 128:(i + 1) * 128, :])
                wt("vector", "sync", ld)
                v1 = vop(lambda eng: eng.tensor_copy(arD[:, 0:2048], arF[:, 0:2048]))
                wt("sync", "v", v1)
                dma("sync", dbg["dbg_kvT"][i * 128:(i + 1) * 128, :], arD[:, 0:2048])
            barrier()

        prev_attn_wr = 0
        prev_vp = 0
        prev_pp = 0
        for chunk in range(16):
            wt("sync", "v", prev_vp)
            wt("sync", "p", prev_pp)
            l1 = dma_t("sync", sqa[:], qkvT_d[:, chunk * 128:(chunk + 1) * 128])
            l2 = dma_t("sync", sqb[:], kvT_d[:, chunk * 128:(chunk + 1) * 128])
            wt("vector", "sync", l2)
            for h in range(NH):
                g, m = h // QPK, h % QPK
                qof = g * 768 + m * 128
                kof_own = g * 768 + 512
                for j in range(4):
                    if j < 3:
                        ksl = sqb[:, j * 1024 + g * 256: j * 1024 + g * 256 + 128]
                    else:
                        ksl = sqa[:, kof_own:kof_own + 128]
                    vop(lambda eng, qof=qof, ksl=ksl, h=h, j=j:
                        eng.scalar_tensor_tensor(satt[:], sqa[:, qof:qof + 128], ATT_SCALE,
                                                 ksl, ALU.mult, ALU.mult,
                                                 accum_out=ssc[:, j * 16 + h: j * 16 + h + 1]))
            vop(lambda eng: eng.tensor_tensor(ssc[:], ssc[:], mv_sb[:], ALU.mult))
            vop(lambda eng: eng.tensor_tensor(ssc[:], ssc[:], ma_sb[:], ALU.add))
            vop(lambda eng: eng.tensor_tensor(ssm[:, 0:16], ssc[:, 0:16], ssc[:, 16:32], ALU.max))
            vop(lambda eng: eng.tensor_tensor(ssm[:, 16:32], ssc[:, 32:48], ssc[:, 48:64], ALU.max))
            vop(lambda eng: eng.tensor_tensor(ssm[:, 0:16], ssm[:, 0:16], ssm[:, 16:32], ALU.max))
            for j in range(4):
                vop(lambda eng, j=j: eng.tensor_tensor(
                    ssc[:, j * 16:(j + 1) * 16], ssc[:, j * 16:(j + 1) * 16], ssm[:, 0:16], ALU.subtract))
            v_pre_e = cnt["v"]
            wt("scalar", "v", v_pre_e)
            s_e = sop(lambda eng: eng.activation(ssc[:], ssc[:], AF.Exp))
            wt("vector", "s", s_e)
            vop(lambda eng: eng.tensor_tensor(ssm[:, 16:32], ssc[:, 0:16], ssc[:, 16:32], ALU.add))
            vop(lambda eng: eng.tensor_tensor(ssm[:, 32:48], ssc[:, 32:48], ssc[:, 48:64], ALU.add))
            vop(lambda eng: eng.tensor_tensor(ssm[:, 16:32], ssm[:, 16:32], ssm[:, 32:48], ALU.add))
            vop(lambda eng: eng.reciprocal(ssm[:, 16:32], ssm[:, 16:32]))
            for j in range(4):
                vop(lambda eng, j=j: eng.tensor_tensor(
                    ssc[:, j * 16:(j + 1) * 16], ssc[:, j * 16:(j + 1) * 16], ssm[:, 16:32], ALU.mult))
            prev_vp = cnt["v"]
            if debug and chunk == 0:
                wt("sync", "v", prev_vp)
                dma("sync", dbg["dbg_probs"], ssc[:])
            wt("gpsimd", "v", prev_vp)
            wt("gpsimd", "sync", prev_attn_wr)
            for h in range(NH):
                g = h // QPK
                vof_own = g * 768 + 640
                for j in range(4):
                    if j < 3:
                        vsl = sqb[:, j * 1024 + g * 256 + 128: j * 1024 + g * 256 + 256]
                    else:
                        vsl = sqa[:, vof_own:vof_own + 128]
                    if j == 0:
                        pop_(lambda eng, h=h, vsl=vsl: eng.tensor_scalar(
                            sso[:, h * 128:(h + 1) * 128], vsl, ssc[:, h:h + 1], None, ALU.mult))
                    else:
                        pop_(lambda eng, h=h, j=j, vsl=vsl: eng.tensor_scalar(
                            ptmp[:], vsl, ssc[:, j * 16 + h:j * 16 + h + 1], None, ALU.mult))
                        pop_(lambda eng, h=h: eng.tensor_tensor(
                            sso[:, h * 128:(h + 1) * 128], sso[:, h * 128:(h + 1) * 128], ptmp[:], ALU.add))
            prev_pp = cnt["p"]
            wt("sync", "p", prev_pp)
            prev_attn_wr = dma("sync", attn_d[chunk * 128:(chunk + 1) * 128, :], sso[:])
        barrier()
        if debug:
            for i in range(16):
                ld = dma("sync", arF[:, 0:2048], attn_d[i * 128:(i + 1) * 128, :])
                wt("vector", "sync", ld)
                v1 = vop(lambda eng: eng.tensor_copy(arD[:, 0:2048], arF[:, 0:2048]))
                wt("sync", "v", v1)
                dma("sync", dbg["dbg_attn"][i * 128:(i + 1) * 128, :], arD[:, 0:2048])
            barrier()

        for hc in range(16):
            dma_t("sync", arA[:, hc * TOK:(hc + 1) * TOK], attn_d[:, hc * 128:(hc + 1) * 128])
        barrier()
        hstage = [arE[:, 2048 + i * 512:2048 + (i + 1) * 512] for i in range(2)]
        hstage_wr = [0, 0]
        last_proj_mm = 0
        for hs in range(4):
            wt("scalar", "t", last_proj_mm)
            lds = []
            for hc in range(16):
                lds.append(dma("scalar", arC[:, hc * 512:(hc + 1) * 512],
                               projwT_d[hc * 128:(hc + 1) * 128, hs * 512:(hs + 1) * 512]))
            wt("tensor", "sc", lds[-1])
            for t128 in range(16):
                si = t128 % 2
                wt("sync", "sync", hstage_wr[si])
                lx = dma("sync", hstage[si], xn_own_d[t128 * 128:(t128 + 1) * 128, hs * 512:(hs + 1) * 512])
                wt("vector", "sync", lx)
                mms = [(lambda hc=hc, t128=t128: arA[:, hc * TOK + t128 * 128: hc * TOK + (t128 + 1) * 128],
                        lambda hc=hc: arC[:, hc * 512:(hc + 1) * 512]) for hc in range(16)]
                def ev(eng, ps, si=si):
                    return eng.scalar_tensor_tensor(hstage[si], ps[:], 1.0, hstage[si], ALU.mult, ALU.add)
                etok, mmtok = mm_group(t128 % 2, mms, ("v", ev))
                last_proj_mm = mmtok
                wt("sync", "v", etok)
                wr = dma("sync", haa_d[t128 * 128:(t128 + 1) * 128, hs * 512:(hs + 1) * 512], hstage[si])
                hstage_wr[si] = wr
        barrier()

        emit_gamma(ln2_w, ln2_b)
        barrier()
        prev_x_wr = 0
        for i in range(16):
            ld = dma("sync", arD[:, 0:2048], haa_d[i * 128:(i + 1) * 128, :])
            wt("vector", "sync", ld)
            for q in range(4):
                vop(lambda eng, q=q: eng.bn_stats(bn6[:, q * 6:(q + 1) * 6], arD[:, q * 512:(q + 1) * 512]))
            v2 = vop(lambda eng: eng.bn_aggr(bn2[:], bn6[:]))
            wt("scalar", "v", v2)
            s_q = sop(lambda eng: eng.activation(bn2[:, 1:2], bn2[:, 1:2], AF.Sqrt, bias=EPS))
            wt("vector", "s", s_q)
            vop(lambda eng: eng.reciprocal(bn2[:, 1:2], bn2[:, 1:2]))
            vop(lambda eng: eng.tensor_scalar(bn2[:, 0:1], bn2[:, 0:1], -1.0, None, ALU.mult))
            vop(lambda eng: eng.tensor_scalar(arE[:, 0:2048], arD[:, 0:2048], bn2[:, 0:1], bn2[:, 1:2], ALU.add, ALU.mult))
            vop(lambda eng: eng.tensor_tensor(arE[:, 0:2048], arE[:, 0:2048], gbg[:], ALU.mult))
            vop(lambda eng: eng.tensor_tensor(arE[:, 0:2048], arE[:, 0:2048], gbb[:], ALU.add))
            wt("vector", "sync", prev_x_wr)
            v_c = vop(lambda eng: eng.tensor_copy(arF[:, 0:2048], arE[:, 0:2048]))
            wt("sync", "v", v_c)
            prev_x_wr = dma("sync", xln2_d[i * 128:(i + 1) * 128, :], arF[:, 0:2048])
        barrier()
        for hc in range(16):
            dma_t("sync", arA[:, hc * TOK:(hc + 1) * TOK], xln2_d[:, hc * 128:(hc + 1) * 128])
        barrier()
        for t128 in range(16):
            mms = [(lambda hc=hc, t128=t128: arA[:, hc * TOK + t128 * 128: hc * TOK + (t128 + 1) * 128],
                    lambda hc=hc: routw_sb[:, hc * NE:(hc + 1) * NE]) for hc in range(16)]
            def ev(eng, ps):
                return eng.tensor_copy(ssm[:, 0:8], ps[:, 0:8])
            mm_group(4, mms, ("v", ev), nn=NE)
            vop(lambda eng: eng.tensor_reduce(ssm[:, 8:9], ssm[:, 0:8], mybir.AxisListType.X, ALU.max, negate=True))
            v3 = vop(lambda eng: eng.tensor_scalar(ssm[:, 0:8], ssm[:, 0:8], ssm[:, 8:9], None, ALU.add))
            wt("scalar", "v", v3)
            s_e = sop(lambda eng: eng.activation(ssm[:, 0:8], ssm[:, 0:8], AF.Exp))
            wt("vector", "s", s_e)
            vop(lambda eng: eng.tensor_reduce(ssm[:, 8:9], ssm[:, 0:8], mybir.AxisListType.X, ALU.add))
            vop(lambda eng: eng.reciprocal(ssm[:, 8:9], ssm[:, 8:9]))
            vop(lambda eng: eng.tensor_scalar(ssm[:, 0:8], ssm[:, 0:8], ssm[:, 8:9], None, ALU.mult))
            vop(lambda eng: eng.max(ssm[:, 16:24], ssm[:, 0:8]))
            vop(lambda eng: eng.tensor_tensor(ssm[:, 24:25], ssm[:, 16:17], ssm[:, 17:18], ALU.add))
            vop(lambda eng: eng.reciprocal(ssm[:, 24:25], ssm[:, 24:25]))
            vop(lambda eng: eng.tensor_scalar(ssm[:, 32:40], ssm[:, 0:8], ssm[:, 17:18], None, ALU.is_ge))
            vop(lambda eng: eng.tensor_tensor(ssm[:, 32:40], ssm[:, 32:40], ssm[:, 0:8], ALU.mult))
            vop(lambda eng, t128=t128: eng.tensor_scalar(
                gates_sb[:, t128 * NE:(t128 + 1) * NE], ssm[:, 32:40], ssm[:, 24:25], None, ALU.mult))
        barrier()
        if debug:
            v1 = vop(lambda eng: eng.tensor_copy(arD[:, 0:128], gates_sb[:]))
            wt("sync", "v", v1)
            dma("sync", dbg["dbg_gates"], arD[:, 0:128])
            for i in range(16):
                ld = dma("sync", arF[:, 0:2048], xln2_d[i * 128:(i + 1) * 128, :])
                wt("vector", "sync", ld)
                v1 = vop(lambda eng: eng.tensor_copy(arD[:, 0:2048], arF[:, 0:2048]))
                wt("sync", "v", v1)
                dma("sync", dbg["dbg_xln2"][i * 128:(i + 1) * 128, :], arD[:, 0:2048])
            for i in range(16):
                ld = dma("sync", arD[:, 0:2048], haa_d[i * 128:(i + 1) * 128, :])
                wt("sync", "sync", ld)
                dma("sync", dbg["dbg_haa"][i * 128:(i + 1) * 128, :], arD[:, 0:2048])
            barrier()

        dma("sync", sel_re, sel_bc)
        n_bct = dma("sync", bct_d2 := bct_re, bct_d[:, :])
        wt("tensor", "sync", n_bct)
        v_z = vop(lambda eng: eng.memset(mst[:], 0.0))
        prev_cx_dma = {}
        prev_scan_of_abuf = {}
        last_cx_v = 0
        last_sa_s = 0
        for tcg in range(NTG):
            wt("sync", "v", last_cx_v)
            wt("sync", "s", last_sa_s)
            lds = []
            for hc in range(8):
                lds.append(dma("sync", mdt[:, hc * TG:(hc + 1) * TG],
                               dt_d[hc * 128:(hc + 1) * 128, tcg * TG:(tcg + 1) * TG]))
                lds.append(dma("sync", mdu[:, hc * TG:(hc + 1) * TG],
                               du_d[hc * 128:(hc + 1) * 128, tcg * TG:(tcg + 1) * TG]))
            dtld = lds[-1]
            wt("scalar", "sync", dtld)
            wt("vector", "sync", dtld)
            for n in range(NST):
                wt("scalar", "v", last_cx_v)
                for half in range(2):
                    ch, nn = bank_tok[6]
                    wt("tensor", ch, nn)
                    tkB = top(lambda eng, n=n, half=half, tcg=tcg: eng.matmul(
                        pb[6][:], sel_re[:, n * 128:(n + 1) * 128],
                        bct_re[:, tcg * TG + half * 512: tcg * TG + (half + 1) * 512],
                        start=True, stop=True))
                    wt("scalar", "t", tkB)
                    bank_tok[6] = ("s", sop(lambda eng, half=half: eng.copy(
                        mbc[:, half * 512:(half + 1) * 512], pb[6][:])))
                    ch, nn = bank_tok[5]
                    wt("tensor", ch, nn)
                    tkC = top(lambda eng, n=n, half=half, tcg=tcg: eng.matmul(
                        pb[5][:], sel_re[:, (16 + n) * 128:(17 + n) * 128],
                        bct_re[:, tcg * TG + half * 512: tcg * TG + (half + 1) * 512],
                        start=True, stop=True))
                    wt("scalar", "t", tkC)
                    bank_tok[5] = ("s", sop(lambda eng, half=half: eng.copy(
                        mbc[:, TG + half * 512: TG + (half + 1) * 512], pb[5][:])))
                bc_tok = cnt["s"]
                for hc in range(8):
                    it = n * 8 + hc
                    bi = it % 2
                    wt("scalar", "v", prev_scan_of_abuf.get(bi, 0))
                    s_a = sop(lambda eng, hc=hc, n=n, bi=bi: eng.activation(
                        ma_[:, bi * TG:(bi + 1) * TG], mdt[:, hc * TG:(hc + 1) * TG],
                        AF.Exp, scale=a_sb[:, hc * NST + n: hc * NST + n + 1]))
                    last_sa_s = s_a
                    wt("vector", "s", bc_tok)
                    wt("vector", "gp", prev_cx_dma.get(bi, 0))
                    v_bu = vop(lambda eng, hc=hc, bi=bi: eng.tensor_tensor(
                        mbu[:, bi * TG:(bi + 1) * TG], mdu[:, hc * TG:(hc + 1) * TG],
                        mbc[:, 0:TG], ALU.mult))
                    wt("vector", "s", s_a)
                    v_sc = vop(lambda eng, hc=hc, n=n, bi=bi: eng.tensor_tensor_scan(
                        mx[:, bi * TG:(bi + 1) * TG], ma_[:, bi * TG:(bi + 1) * TG],
                        mbu[:, bi * TG:(bi + 1) * TG],
                        mst[:, hc * 16 + n: hc * 16 + n + 1], ALU.mult, ALU.add))
                    prev_scan_of_abuf[bi] = v_sc
                    vop(lambda eng, hc=hc, n=n, bi=bi: eng.tensor_copy(
                        mst[:, hc * 16 + n: hc * 16 + n + 1], mx[:, (bi + 1) * TG - 1:(bi + 1) * TG]))
                    v_cx = vop(lambda eng, bi=bi: eng.tensor_tensor(
                        mcx[:, bi * TG:(bi + 1) * TG], mx[:, bi * TG:(bi + 1) * TG],
                        mbc[:, TG:2 * TG], ALU.mult))
                    last_cx_v = v_cx
                    wt("gpsimd", "v", v_cx)
                    prev_cx_dma[bi] = dma("gpsimd", y_d[hc * 128:(hc + 1) * 128, tcg * TG:(tcg + 1) * TG],
                                          mcx[:, bi * TG:(bi + 1) * TG], accum_op=ALU.add)
                wt("gpsimd", "gp", cnt["gp"])
        barrier()
        prev_v = 0
        prev_w = 0
        for hc in range(8):
            for tc in range(4):
                wt("sync", "v", prev_v)
                ld = dma("sync", arD[:, 0:1024], y_d[hc * 128:(hc + 1) * 128, tc * 1024:(tc + 1) * 1024])
                wt("vector", "sync", ld)
                wt("vector", "sync", prev_w)
                prev_v = vop(lambda eng: eng.tensor_copy(arF[:, 0:1024], arD[:, 0:1024]))
                wt("sync", "v", prev_v)
                prev_w = dma("sync", y_bf_d[hc * 128:(hc + 1) * 128, tc * 1024:(tc + 1) * 1024], arF[:, 0:1024])
        barrier()
        tprev = {0: 0, 1: 0}
        for tc in range(32):
            bi = tc % 2
            wt("sync", "sync", tprev[bi])
            t1 = dma_t("sync", arF[:, bi * 1024:(bi + 1) * 1024], y_bf_d[:, tc * 128:(tc + 1) * 128])
            wt("sync", "sync", t1)
            tprev[bi] = dma("sync", y_tok_d[tc * 128:(tc + 1) * 128, :], arF[:, bi * 1024:(bi + 1) * 1024])
        barrier()
        if debug:
            for i in range(32):
                ld = dma("sync", arF[:, 0:1024], y_tok_d[i * 128:(i + 1) * 128, :])
                wt("vector", "sync", ld)
                v1 = vop(lambda eng: eng.tensor_copy(arD[:, 0:1024], arF[:, 0:1024]))
                wt("sync", "v", v1)
                dma("sync", dbg["dbg_ytok"][i * 128:(i + 1) * 128, :], arD[:, 0:1024])
            barrier()

        gp_base = cnt["gp"]
        def fn_xchg(eng, gp_base=gp_base):
            rk = eng.value_load(rank_in)
            k = rk % 2
            for body in eng.Switch(k, 2):
                other = 1 - body
                eng.dma_start(out=xchg_y[body, :, :],
                              in_=y_tok_d[other * TOK:(other + 1) * TOK, :]).then_inc(sm_gp, 16)
                eng.dma_start(out=y_cat_d[:, body * HH:(body + 1) * HH],
                              in_=y_tok_d[body * TOK:(body + 1) * TOK, :]).then_inc(sm_gp, 16)
                eng.wait_ge(sm_gp, 16 * (gp_base + 2))
                eng.store(xchg_fl[body:body + 1, 0:1], MAGIC)
                with eng.register(f"xf{body}") as f:
                    eng.reg_mov(f, 1)
                    with eng.While(f):
                        eng.load(f, xchg_fl[other:other + 1, 0:1])
                        eng.reg_alu(f, f, MAGIC, ALU.bitwise_xor)
                eng.dma_start(out=y_cat_d[:, other * HH:(other + 1) * HH],
                              in_=xchg_y[other, :, :]).then_inc(sm_gp, 16)
            eng.wait_ge(sm_gp, 16 * (gp_base + 3))
            eng.nop().then_inc(sm_p, 1)
        on("gpsimd", fn_xchg)
        cnt["gp"] += 3
        cnt["p"] += 1
        xchg_done_p = cnt["p"]

        last_w2_mm = 0
        last_h1_mm = 0
        prev_estage_gp = [0, 0, 0, 0]
        estage = [arE[:, i * 512:(i + 1) * 512] for i in range(4)]
        for e in range(NE):
            for fhalf in range(2):
                for sub in range(2):
                    f0 = fhalf * 2048 + sub * 1024
                    wt("gpsimd", "t", last_h1_mm)
                    wt("gpsimd", "t", last_w2_mm)
                    w1lds = []
                    for hc in range(16):
                        w1lds.append(dma("gpsimd", arC[:, hc * 1024:(hc + 1) * 1024],
                                         moe_w1[e, hc * 128:(hc + 1) * 128, f0:f0 + 1024]))
                    wt("tensor", "gp", w1lds[-1])
                    wt("scalar", "t", last_w2_mm)
                    for fc in range(8):
                        for tcol in range(4):
                            mms = [(lambda hc=hc, fc=fc: arC[:, hc * 1024 + fc * 128: hc * 1024 + (fc + 1) * 128],
                                    lambda hc=hc, tcol=tcol: arA[:, hc * TOK + tcol * 512: hc * TOK + (tcol + 1) * 512])
                                   for hc in range(16)]
                            grow_ = sub * 8 + fc
                            def ev(eng, ps, grow_=grow_, tcol=tcol):
                                return eng.activation(
                                    arB[:, grow_ * TOK + tcol * 512: grow_ * TOK + (tcol + 1) * 512],
                                    ps[:], AF.Gelu)
                            etok, mmtok = mm_group((fc * 4 + tcol) % 2, mms, ("s", ev))
                            last_h1_mm = mmtok
                g_done = cnt["s"]
                for hs in range(4):
                    wt("gpsimd", "t", last_w2_mm)
                    wt("gpsimd", "t", last_h1_mm)
                    w2lds = []
                    for fc in range(16):
                        w2lds.append(dma("gpsimd", arC[:, fc * 512:(fc + 1) * 512],
                                         moe_w2[e, fhalf * 2048 + fc * 128: fhalf * 2048 + (fc + 1) * 128,
                                                hs * 512:(hs + 1) * 512]))
                    wt("tensor", "gp", w2lds[-1])
                    wt("tensor", "s", g_done)
                    for t128 in range(16):
                        si = t128 % 4
                        mms = [(lambda fc=fc, t128=t128: arB[:, fc * TOK + t128 * 128: fc * TOK + (t128 + 1) * 128],
                                lambda fc=fc: arC[:, fc * 512:(fc + 1) * 512]) for fc in range(16)]
                        wt("scalar", "gp", prev_estage_gp[si])
                        def ev(eng, ps, e=e, t128=t128, si=si):
                            return eng.activation(estage[si], ps[:], AF.Copy,
                                                  scale=gates_sb[:, t128 * NE + e: t128 * NE + e + 1])
                        etok, mmtok = mm_group(2 + (t128 % 2), mms, ("s", ev))
                        last_w2_mm = mmtok
                        wt("gpsimd", "s", etok)
                        if e == 0 and fhalf == 0:
                            prev_estage_gp[si] = dma("gpsimd",
                                                     moeacc[t128 * 128:(t128 + 1) * 128, hs * 512:(hs + 1) * 512],
                                                     estage[si])
                        else:
                            prev_estage_gp[si] = dma("gpsimd",
                                                     moeacc[t128 * 128:(t128 + 1) * 128, hs * 512:(hs + 1) * 512],
                                                     estage[si], accum_op=ALU.add)
                    wt("gpsimd", "gp", cnt["gp"])
        barrier()
        if debug:
            for i in range(16):
                ld = dma("sync", arD[:, 0:2048], moeacc[i * 128:(i + 1) * 128, :])
                wt("sync", "sync", ld)
                dma("sync", dbg["dbg_moe"][i * 128:(i + 1) * 128, :], arD[:, 0:2048])
            barrier()

        wt("vector", "p", xchg_done_p)
        wt("sync", "p", xchg_done_p)
        prev_out = 0
        prev_y_v = 0
        for t128 in range(16):
            wt("sync", "v", prev_y_v)
            wt("sync", "sync", prev_out)
            l1 = dma("sync", arD[:, 0:2048], x_own[t128 * 128:(t128 + 1) * 128, :])
            l2 = dma("sync", arD[:, 2048:4096], haa_d[t128 * 128:(t128 + 1) * 128, :])
            l3 = dma("sync", arE[:, 0:2048], moeacc[t128 * 128:(t128 + 1) * 128, :])
            l4 = dma("sync", arF[:, 0:2048], y_cat_d[t128 * 128:(t128 + 1) * 128, :])
            wt("vector", "sync", l4)
            vop(lambda eng: eng.tensor_tensor(arD[:, 0:2048], arD[:, 0:2048], arD[:, 2048:4096], ALU.add))
            vop(lambda eng: eng.tensor_tensor(arD[:, 0:2048], arD[:, 0:2048], arE[:, 0:2048], ALU.add))
            prev_y_v = vop(lambda eng: eng.tensor_tensor(arD[:, 0:2048], arD[:, 0:2048], arF[:, 0:2048], ALU.add))
            wt("sync", "v", prev_y_v)
            prev_out = dma("sync", out_slab[t128 * 128:(t128 + 1) * 128, :], arD[:, 0:2048])
        wt("sync", "sync", cnt["sync"])
        wt("gpsimd", "gp", cnt["gp"])
        barrier()

        block = ctx.enter_context(nc.Block())

        @block.sync
        def _(eng):
            for fn in rec["sync"]:
                fn(eng)

        @block.gpsimd
        def _(eng):
            for fn in rec["gpsimd"]:
                fn(eng)

        @block.vector
        def _(eng):
            for fn in rec["vector"]:
                fn(eng)

        @block.scalar
        def _(eng):
            for fn in rec["scalar"]:
                fn(eng)

        @block.tensor
        def _(eng):
            for fn in rec["tensor"]:
                fn(eng)

    return nc



def _host_inputs(inputs, c):
    b, k = c // 2, c % 2
    x = np.asarray(inputs["L0_attn_input"], np.float32).reshape(B, S, H)
    perm = np.concatenate([np.arange(k * HH, (k + 1) * HH), np.arange((1 - k) * HH, (2 - k) * HH)])
    s0 = k * TOK
    slots = [bb for bb in range(B) if bb != b]
    mv = np.zeros((128, 64), np.float32)
    mam = np.zeros((128, 64), np.float32)
    for j in range(4):
        bj = slots[j] if j < 3 else b
        if bj <= b:
            mv[:, j * 16:(j + 1) * 16] = 1.0
        else:
            mam[:, j * 16:(j + 1) * 16] = NEG
    h0 = k * HH
    sel = np.zeros((32, 32 * 128), ml_dtypes.bfloat16)
    for i in range(32):
        sel[i, i * 128:(i + 1) * 128] = 1.0
    return {
        "x_own": np.ascontiguousarray(x[b, s0:s0 + TOK, :]),
        "x_bb": np.ascontiguousarray(x[b][:, perm]),
        "x_ext": np.ascontiguousarray(np.concatenate([x[bb, s0:s0 + TOK, :] for bb in slots], 0)),
        "wdtin_p": np.ascontiguousarray(np.asarray(inputs["L0_attn_W_dt_in"], np.float32)[:, perm]),
        "wbc_p": np.ascontiguousarray(np.concatenate(
            [np.asarray(inputs["L0_attn_W_B"], np.float32),
             np.asarray(inputs["L0_attn_W_C"], np.float32)], 0)[:, perm]),
        "wdtout_h": np.ascontiguousarray(np.asarray(inputs["L0_attn_W_dt_out"], np.float32)[h0:h0 + HH, :]),
        "bdt_h": np.ascontiguousarray(np.asarray(inputs["L0_attn_b_dt_out"], np.float32)[h0:h0 + HH]),
        "alog_h": np.ascontiguousarray(np.asarray(inputs["L0_attn_A_log"], np.float32)[h0:h0 + HH, :]),
        "d_h": np.ascontiguousarray(np.asarray(inputs["L0_attn_D"], np.float32)[h0:h0 + HH]),
        "ln1_w": np.asarray(inputs["L0_ffn_ln1_weight"], np.float32),
        "ln1_b": np.asarray(inputs["L0_ffn_ln1_bias"], np.float32),
        "ln2_w": np.asarray(inputs["L0_ffn_ln2_weight"], np.float32),
        "ln2_b": np.asarray(inputs["L0_ffn_ln2_bias"], np.float32),
        "qkv_w": np.asarray(inputs["L0_ffn_qkv_weight"], np.float32),
        "proj_w": np.asarray(inputs["L0_ffn_proj_weight"], np.float32),
        "router_w": np.asarray(inputs["L0_ffn_router_weight"], np.float32),
        "moe_w1": np.asarray(inputs["L0_ffn_moe_w1"], np.float32),
        "moe_w2": np.asarray(inputs["L0_ffn_moe_w2"], np.float32),
        "rank": np.array([[c]], np.int32),
        "eye_bf": np.eye(128, dtype=ml_dtypes.bfloat16),
        "ones_row": np.ones((1, 128), ml_dtypes.bfloat16),
        "sel_bc": sel,
        "mv_t": mv,
        "ma_t": mam,
    }


_NC_CACHE = {}


def kernel(**inputs):
    key = ("v1", DEBUG)
    if key not in _NC_CACHE:
        _NC_CACHE[key] = build_nc(DEBUG)
    nc = _NC_CACHE[key]
    in_maps = [_host_inputs(inputs, c) for c in range(NCORES)]
    res = run_bass_kernel_spmd(nc, in_maps, core_ids=list(range(NCORES)))
    out = np.zeros((B, S, H), np.float32)
    for c in range(NCORES):
        b, k = c // 2, c % 2
        out[b, k * TOK:(k + 1) * TOK, :] = res.results[c]["out_slab"]
    return out


# version 06
# speedup vs baseline: 1.4830x; 1.4830x over previous
import numpy as np
import ml_dtypes
from contextlib import ExitStack

import concourse.bass as bass
import concourse.mybir as mybir
from concourse.bass_utils import run_bass_kernel_spmd

F32 = mybir.dt.float32
BF16 = mybir.dt.bfloat16
I32 = mybir.dt.int32
AF = mybir.ActivationFunctionType
ALU = mybir.AluOpType

B, S, H = 4, 4096, 2048
NTOK = B * S
DTR, NST = 64, 16
NE, FF = 8, 4096
NH, NKV, HD = 16, 4, 128
QPK = NH // NKV
OD = NKV * (QPK + 2) * HD
EPS = 1e-5
TOK = 2048
HH = 1024
NCORES = 8
MAGIC = 0x5CA1AB1E
NEG = np.float32(-1.0e9)
ATT_SCALE = float(np.float32(1.0) / np.sqrt(np.float32(HD)))

DEBUG = False


def build_nc(debug=DEBUG):
    nc = bass.Bass("TRN2", target_bir_lowering=False)

    def din(name, shape, dt=F32):
        return nc.dram_tensor(name, shape, dt, kind="ExternalInput").ap()

    x_own = din("x_own", [TOK, H])
    x_bb = din("x_bb", [S, H])
    x_ext = din("x_ext", [3 * TOK, H])
    wdtin_p = din("wdtin_p", [DTR, H])
    wbc_p = din("wbc_p", [2 * NST, H])
    wdtout_h = din("wdtout_h", [HH, DTR])
    bdt_h = din("bdt_h", [HH])
    alog_h = din("alog_h", [HH, NST])
    d_h = din("d_h", [HH])
    ln1_w = din("ln1_w", [H])
    ln1_b = din("ln1_b", [H])
    ln2_w = din("ln2_w", [H])
    ln2_b = din("ln2_b", [H])
    qkv_w = din("qkv_w", [OD, H])
    proj_w = din("proj_w", [H, H])
    router_w = din("router_w", [H, NE])
    moe_w1 = din("moe_w1", [NE, H, FF])
    moe_w2 = din("moe_w2", [NE, FF, H])
    rank_in = din("rank", [1, 1], I32)
    eye_bf = din("eye_bf", [128, 128], BF16)
    ones_row = din("ones_row", [1, 128], BF16)
    sel_bc = din("sel_bc", [32, 32 * 128], BF16)
    mv_t = din("mv_t", [128, 64])
    ma_t = din("ma_t", [128, 64])

    out_slab = nc.dram_tensor("out_slab", [TOK, H], F32, kind="ExternalOutput").ap()
    dbg = {}
    if debug:
        def dout(name, shape, dt=F32):
            dbg[name] = nc.dram_tensor(name, shape, dt, kind="ExternalOutput").ap()
        dout("dbg_xn_own", [TOK, H])
        dout("dbg_dt", [HH, S])
        dout("dbg_bct", [32, S])
        dout("dbg_ytok", [S, HH])
        dout("dbg_attn", [TOK, H])
        dout("dbg_ln1", [TOK, H])
        dout("dbg_qkvT", [OD, TOK])
        dout("dbg_kvT", [3 * 1024, TOK])
        dout("dbg_probs", [128, 64])
        dout("dbg_dtpre", [DTR, S])
        dout("dbg_wdi", [128, 16 * DTR])
        dout("dbg_wdi_t1", [128, 16 * DTR])
        dout("dbg_wdi_t2", [128, 16 * DTR])
        dout("dbg_haa", [TOK, H])
        dout("dbg_xln2", [TOK, H])
        dout("dbg_gates", [128, 128])
        dout("dbg_moe", [TOK, H])

    def scr(name, shape, dt):
        return nc.dram_tensor(name, shape, dt).ap()

    xn_bb = scr("xn_bb", [S, H], BF16)
    xnT_bb = scr("xnT_bb", [H, S], BF16)
    xn_own_d = scr("xn_own_d", [TOK, H], F32)
    ln1_own_d = scr("ln1_own_d", [TOK, H], BF16)
    ln1_ext_d = scr("ln1_ext_d", [3 * TOK, H], BF16)
    qkvw_bf = scr("qkvw_bf", [OD, H], BF16)
    qkvwT_d = scr("qkvwT_d", [H, OD], BF16)
    projw_bf = scr("projw_bf", [H, H], BF16)
    projwT_d = scr("projwT_d", [H, H], BF16)
    dt_d = scr("dt_d", [HH, S], BF16)
    du_d = scr("du_d", [HH, S], BF16)
    y_d = scr("y_d", [HH, S], F32)
    y_bf_d = scr("y_bf_d", [HH, S], BF16)
    y_tok_d = scr("y_tok_d", [S, HH], BF16)
    y_par_d = scr("y_par_d", [TOK, HH], BF16)
    qkvT_d = scr("qkvT_d", [OD, TOK], BF16)
    kvT_d = scr("kvT_d", [3 * 1024, TOK], BF16)
    attn_d = scr("attn_d", [TOK, H], BF16)
    haa_d = scr("haa_d", [TOK, H], F32)
    xln2_d = scr("xln2_d", [TOK, H], BF16)
    moeacc = scr("moeacc", [TOK, H], F32)
    bct_d = scr("bct_d", [32, S], BF16)
    y_cat_d = scr("y_cat_d", [TOK, H], BF16)

    xchg_y = nc.dram_tensor("xchg_y", [2, TOK, HH], BF16, addr_space="Shared").ap()
    xchg_fl = nc.dram_tensor("xchg_fl", [2, 16], I32, addr_space="Shared").ap()

    with ExitStack() as ctx:
        def sbt(name, p, f, dt):
            return ctx.enter_context(nc.sbuf_tensor(name, [p, f], dt))

        def sem(name):
            return ctx.enter_context(nc.semaphore(name))

        arA = sbt("arA", 128, 32768, BF16)
        arB = sbt("arB", 128, 32768, BF16)
        arC = sbt("arC", 128, 16384, BF16)
        arD = sbt("arD", 128, 4096, F32)
        arE = sbt("arE", 128, 4096, F32)
        arF = sbt("arF", 128, 4096, BF16)
        bn6 = sbt("bn6", 128, 24, F32)
        bn2 = sbt("bn2", 128, 2, F32)
        gbg = arA[:, 0:2048]
        gbb = arA[:, 2048:4096]
        stats = arA[:, 4096:5056].bitcast(F32)
        onesr_sb = sbt("onesr_sb", 1, 128, BF16)
        mv_sb = sbt("mv_sb", 128, 64, F32)
        ma_sb = sbt("ma_sb", 128, 64, F32)
        a_sb = sbt("a_sb", 128, 8 * NST, F32)
        d_sb = sbt("d_sb", 128, 8, F32)
        bdt_sb = sbt("bdt_sb", 128, 8, F32)
        routw_sb = sbt("routw_sb", 128, 16 * NE, BF16)
        cpool = sbt("cpool", 128, 4, F32)
        wdtinT_ded = sbt("wdtinT_ded", 128, 16 * DTR, BF16)
        gates_sb = sbt("gates_sb", 128, 16 * NE, F32)
        wdtinT_sb = wdtinT_ded[:, :]
        wbcT_sb = arC[:, 1024:1024 + 16 * 32]
        wdtoutT_sb = arC[0:DTR, 1536:1536 + HH]
        wdi_sb = arC[0:DTR, 2560:2560 + H]
        wbc_sb = arC[0:32, 4608:4608 + H]
        wdo_sb = arC[:, 6656:6656 + 8 * DTR]
        eye_sb = arC[:, 7168:7296]
        dtpre_bf = arC[0:DTR, 8192:8192 + S]
        sel_re = arC[0:32, 0:4096]
        bct_re = arC[0:32, 4096:4096 + S]
        TG = 1024
        NTG = S // TG
        mdt = arB[:, 0:8 * TG]
        mdu = arB[:, 8 * TG:16 * TG]
        mbc = arB[:, 16384:16384 + 2 * TG]
        ma_ = arB[:, 18432:18432 + 2 * TG]
        mbu = arB[:, 20480:20480 + 2 * TG]
        mx = arB[:, 22528:22528 + 2 * TG]
        mcx = arB[:, 24576:24576 + 2 * TG]
        mst = arB[:, 26624:26880].bitcast(F32)
        sqa = arB[:, 0:3072]
        sqb = arB[:, 3072:6144]
        sso = arB[:, 6144:8192]
        satt = arB[:, 8192:8320]
        ssc = arB[:, 8320:8448].bitcast(F32)
        ssm = arB[:, 8448:8544].bitcast(F32)
        ptmp = arB[:, 8576:8704]
        pb = [ctx.enter_context(nc.psum_tensor(f"pb{i}", [128, 512], F32)) for i in range(7)]
        pbf7 = ctx.enter_context(nc.psum_tensor("pbf7", [128, 1024], BF16))

        sm_sync = sem("sm_sync")
        sm_gp = sem("sm_gp")
        sm_sc = sem("sm_sc")
        sm_v = sem("sm_v")
        sm_s = sem("sm_s")
        sm_p = sem("sm_p")
        sm_t = sem("sm_t")
        sm_bar = sem("sm_bar")

        cnt = {"sync": 0, "gp": 0, "sc": 0, "v": 0, "s": 0, "p": 0, "t": 0, "bar": 0}
        rec = {"sync": [], "gpsimd": [], "vector": [], "scalar": [], "tensor": []}

        def on(engine, fn):
            rec[engine].append(fn)

        QSEM = {"sync": sm_sync, "gpsimd": sm_gp, "scalar": sm_sc}
        QCNT = {"sync": "sync", "gpsimd": "gp", "scalar": "sc"}
        CSEM = {"v": sm_v, "s": sm_s, "t": sm_t, "p": sm_p,
                "sync": sm_sync, "gp": sm_gp, "sc": sm_sc}
        ENG_OF = {"v": "vector", "s": "scalar", "p": "gpsimd", "t": "tensor"}

        def dma(queue, out, in_, **kw):
            cnt[QCNT[queue]] += 1
            n = cnt[QCNT[queue]]
            s = QSEM[queue]
            def fn(eng, out=out, in_=in_, kw=dict(kw), s=s):
                eng.dma_start(out=out, in_=in_, **kw).then_inc(s, 16)
            on(queue, fn)
            return n

        def dma_t(queue, out, in_):
            cnt[QCNT[queue]] += 1
            n = cnt[QCNT[queue]]
            s = QSEM[queue]
            def fn(eng, out=out, in_=in_, s=s):
                eng.dma_start_transpose(out=out, in_=in_).then_inc(s, 16)
            on(queue, fn)
            return n

        def wt(engine, chain, n):
            if n <= 0:
                return
            s = CSEM[chain]
            mult = 16 if chain in ("sync", "gp", "sc") else 1
            on(engine, lambda eng, s=s, n=n, m=mult: eng.wait_ge(s, m * n))

        def vop(fn):
            cnt["v"] += 1
            n = cnt["v"]
            def run(eng, fn=fn, n=n):
                if n > 1:
                    eng.wait_ge(sm_v, n - 1)
                fn(eng).then_inc(sm_v, 1)
            on("vector", run)
            return n

        def sop(fn):
            cnt["s"] += 1
            n = cnt["s"]
            def run(eng, fn=fn, n=n):
                if n > 1:
                    eng.wait_ge(sm_s, n - 1)
                fn(eng).then_inc(sm_s, 1)
            on("scalar", run)
            return n

        def pop_(fn):
            cnt["p"] += 1
            n = cnt["p"]
            def run(eng, fn=fn, n=n):
                if n > 1:
                    eng.wait_ge(sm_p, n - 1)
                fn(eng).then_inc(sm_p, 1)
            on("gpsimd", run)
            return n

        def top(fn):
            cnt["t"] += 1
            def run(eng, fn=fn):
                fn(eng).then_inc(sm_t, 1)
            on("tensor", run)
            return cnt["t"]

        def top_q(fn):
            on("tensor", lambda eng, fn=fn: fn(eng))

        def barrier():
            cnt["bar"] += 1
            tgt = 5 * cnt["bar"]
            d1, d2, d3 = cnt["sync"], cnt["gp"], cnt["sc"]
            for engname in ("sync", "gpsimd", "vector", "scalar", "tensor"):
                def fn(eng, tgt=tgt, d1=d1, d2=d2, d3=d3):
                    if d1:
                        eng.wait_ge(sm_sync, 16 * d1)
                    if d2:
                        eng.wait_ge(sm_gp, 16 * d2)
                    if d3:
                        eng.wait_ge(sm_sc, 16 * d3)
                    eng.nop().then_inc(sm_bar, 1)
                    eng.wait_ge(sm_bar, tgt)
                on(engname, fn)

        bank_tok = [("v", 0)] * 8

        def mm_group(bank, mms, evac, m=128, nn=512):
            ch, n = bank_tok[bank]
            wt("tensor", ch, n)
            for i, (lf, rf) in enumerate(mms):
                st = (i == 0)
                sp = (i == len(mms) - 1)
                if sp:
                    t_tok = top(lambda eng, lf=lf, rf=rf, st=st, sp=sp:
                                eng.matmul(pb[bank][0:m, 0:nn], lf(), rf(), start=st, stop=sp))
                else:
                    top_q(lambda eng, lf=lf, rf=rf, st=st, sp=sp:
                          eng.matmul(pb[bank][0:m, 0:nn], lf(), rf(), start=st, stop=sp))
            ch2, fn2 = evac
            wt(ENG_OF[ch2], "t", t_tok)
            if ch2 == "v":
                etok = vop(lambda eng, fn2=fn2: fn2(eng, pb[bank]))
            elif ch2 == "s":
                etok = sop(lambda eng, fn2=fn2: fn2(eng, pb[bank]))
            bank_tok[bank] = (ch2, etok)
            return etok, t_tok

        rk_cell = {}

        def get_rank(engine):
            if engine not in rk_cell:
                cell = {}
                rk_cell[engine] = cell
                def fn(eng, cell=cell):
                    cell["rk"] = eng.value_load(rank_in)
                    cell["k"] = cell["rk"] % 2
                on(engine, fn)
            return rk_cell[engine]

        nc.const_aps.register(F32, EPS, cpool[:, 0:1])
        def fn_cpool(eng):
            eng.memset(cpool[:, 0:1], EPS).then_inc(sm_v, 1)
        on("vector", fn_cpool)
        cnt["v"] += 1
        v_cpool = cnt["v"]
        wt("scalar", "v", v_cpool)
        dma("sync", eye_sb[:], eye_bf)
        dma("sync", onesr_sb[:], ones_row)
        dma("sync", mv_sb[:], mv_t)
        dma("sync", ma_sb[:], ma_t)
        for hc in range(8):
            dma("sync", a_sb[:, hc * NST:(hc + 1) * NST], alog_h[hc * 128:(hc + 1) * 128, :])
            dma("sync", d_sb[:, hc:hc + 1], d_h[hc * 128:(hc + 1) * 128][:, None])
            dma("sync", bdt_sb[:, hc:hc + 1], bdt_h[hc * 128:(hc + 1) * 128][:, None])
        n_al = cnt["sync"]
        gamma_last = {"t": 0, "s": 0}
        def emit_gamma(w_vec, b_vec):
            r1 = dma("sync", arD[0:1, 0:H], w_vec[None, :])
            r2 = dma("sync", arD[0:1, 2048:2048 + H], b_vec[None, :])
            wt("vector", "sync", r2)
            vg = vop(lambda eng: eng.tensor_copy(arF[0:1, 0:H], arD[0:1, 0:H]))
            vg = vop(lambda eng: eng.tensor_copy(arF[0:1, H:2 * H], arD[0:1, 2048:2048 + H]))
            wt("tensor", "v", vg)
            wt("tensor", "sync", 2)
            for gi, dst in enumerate([gbg, gbb]):
                for col in range(4):
                    ch, n = bank_tok[6]
                    wt("tensor", ch, n)
                    tk = top(lambda eng, gi=gi, col=col: eng.matmul(
                        pb[6][:], onesr_sb[:],
                        arF[0:1, gi * H + col * 512: gi * H + (col + 1) * 512],
                        start=True, stop=True))
                    wt("scalar", "t", tk)
                    bank_tok[6] = ("s", sop(lambda eng, dst=dst, col=col: eng.copy(
                        dst[:, col * 512:(col + 1) * 512], pb[6][:])))
            gamma_last["t"] = cnt["t"]
            gamma_last["s"] = cnt["s"]
        emit_gamma(ln1_w, ln1_b)
        wt("scalar", "sync", n_al)
        s_a1 = sop(lambda eng: eng.activation(a_sb[:], a_sb[:], AF.Exp))
        wt("vector", "s", s_a1)
        vop(lambda eng: eng.tensor_scalar(a_sb[:], a_sb[:], -1.0, None, ALU.mult))
        for hc in range(16):
            dma("gpsimd", routw_sb[:, hc * NE:(hc + 1) * NE], router_w[hc * 128:(hc + 1) * 128, :])

        def cast_rt(src, dst, rows, cols, tag):
            toks = []
            nblk = rows // 128
            for i in range(nblk):
                bi = i % 2
                bslc = arF[:, bi * cols:(bi + 1) * cols]
                if i >= 2:
                    wt("gpsimd", "gp", toks[i - 2][1])
                t_in = dma("gpsimd", bslc, src[i * 128:(i + 1) * 128, :])
                wt("gpsimd", "gp", t_in)
                t_out = dma("gpsimd", dst[i * 128:(i + 1) * 128, :], bslc)
                toks.append((t_in, t_out))
            return toks[-1][1]

        wt("gpsimd", "t", gamma_last["t"])
        qkv_cast = cast_rt(qkv_w, qkvw_bf, OD, H, "qkv")
        proj_cast = cast_rt(proj_w, projw_bf, H, H, "proj")
        wt("sync", "gp", qkv_cast)
        tq = []
        for hc in range(16):
            bi = hc % 2
            tbb = arB[:, bi * OD:(bi + 1) * OD]
            if hc >= 2:
                wt("sync", "sync", tq[hc - 2])
            t1 = dma_t("sync", tbb, qkvw_bf[:, hc * 128:(hc + 1) * 128])
            wt("sync", "sync", t1)
            tq.append(dma("sync", qkvwT_d[hc * 128:(hc + 1) * 128, :], tbb))
        wt("sync", "gp", proj_cast)
        tp = []
        for hc in range(16):
            bi = hc % 2
            tbb = arB[:, bi * OD:bi * OD + H]
            if hc >= 2:
                wt("sync", "sync", tp[hc - 2])
            t1 = dma_t("sync", tbb, projw_bf[:, hc * 128:(hc + 1) * 128])
            wt("sync", "sync", t1)
            tp.append(dma("sync", projwT_d[hc * 128:(hc + 1) * 128, :], tbb))
        n_wdi = dma("gpsimd", wdi_sb[:], wdtin_p)
        n_wbc = dma("gpsimd", wbc_sb[:], wbc_p)
        for i in range(8):
            dma("gpsimd", wdo_sb[:, i * DTR:(i + 1) * DTR], wdtout_h[i * 128:(i + 1) * 128, :])
        n_wdo = cnt["gp"]
        pbf = pbf7[:]
        wt("tensor", "gp", n_wdo)
        wt("tensor", "sync", 1)
        for hc in range(16):
            ch, n = bank_tok[7]
            wt("tensor", ch, n)
            tk = top(lambda eng, hc=hc: eng.transpose(
                pbf[:, 0:DTR], wdi_sb[0:DTR, hc * 128:(hc + 1) * 128], eye_sb[0:DTR, 0:DTR]))
            wt("scalar", "t", tk)
            bank_tok[7] = ("s", sop(lambda eng, hc=hc: eng.copy(
                wdtinT_sb[:, hc * DTR:(hc + 1) * DTR], pbf[:, 0:DTR])))
        wt("tensor", "gp", n_wbc)
        for hc in range(16):
            ch, n = bank_tok[7]
            wt("tensor", ch, n)
            tk = top(lambda eng, hc=hc: eng.transpose(
                pbf[:, 0:32], wbc_sb[:, hc * 128:(hc + 1) * 128], eye_sb[0:32, 0:32]))
            wt("scalar", "t", tk)
            bank_tok[7] = ("s", sop(lambda eng, hc=hc: eng.copy(
                wbcT_sb[:, hc * 32:(hc + 1) * 32], pbf[:, 0:32])))
        wt("tensor", "gp", n_wdo)
        for i in range(8):
            ch, n = bank_tok[7]
            wt("tensor", ch, n)
            tk = top(lambda eng, i=i: eng.transpose(
                pbf[0:DTR, 0:128], wdo_sb[:, i * DTR:(i + 1) * DTR], eye_sb[:]))
            wt("scalar", "t", tk)
            bank_tok[7] = ("s", sop(lambda eng, i=i: eng.copy(
                wdtoutT_sb[0:DTR, i * 128:(i + 1) * 128], pbf[0:DTR, 0:128])))
        barrier()
        if debug:
            vt1 = vop(lambda eng: eng.tensor_copy(arD[:, 0:1024], wdtinT_sb[:]))
            wt("sync", "v", vt1)
            dma("sync", dbg["dbg_wdi_t1"], arD[:, 0:1024])
            barrier()

        xsrc = [(x_bb, 0, 32, "bb"), (x_own, 32, 16, "own"), (x_ext, 48, 48, "ext")]
        NMU, NRS, NR1, NSC = 0, 96, 192, 384
        xbuf = [arD[:, 0:2048], arD[:, 2048:4096]]
        tile_jobs = []
        for src, colbase, ntile, tag in xsrc:
            for i in range(ntile):
                tile_jobs.append((src, i, colbase + i, tag))
        lastv = {0: 0, 1: 0}
        for j, (src, i, col, tag) in enumerate(tile_jobs):
            bi = j % 2
            wt("sync", "v", lastv[bi])
            ld = dma("sync", xbuf[bi], src[i * 128:(i + 1) * 128, :])
            wt("vector", "sync", ld)
            for q in range(4):
                v1 = vop(lambda eng, bi=bi, q=q: eng.bn_stats(bn6[:, q * 6:(q + 1) * 6], xbuf[bi][:, q * 512:(q + 1) * 512]))
            vop(lambda eng: eng.bn_aggr(bn2[:], bn6[:]))
            vop(lambda eng, col=col: eng.tensor_copy(stats[:, NMU + col:NMU + col + 1], bn2[:, 0:1]))
            vop(lambda eng, col=col: eng.tensor_copy(stats[:, NRS + col:NRS + col + 1], bn2[:, 1:2]))
            lastv[bi] = v1
        vop(lambda eng: eng.tensor_scalar(stats[:, NMU:NMU + 96], stats[:, NMU:NMU + 96], -1.0, None, ALU.mult))
        v_pre = cnt["v"]
        wt("scalar", "v", v_pre)
        s1 = sop(lambda eng: eng.activation(stats[:, NSC:NSC + 96], stats[:, NRS:NRS + 96], AF.Sqrt, bias=EPS))
        wt("vector", "s", s1)
        vop(lambda eng: eng.reciprocal(stats[:, NSC:NSC + 96], stats[:, NSC:NSC + 96]))
        vop(lambda eng: eng.tensor_tensor(stats[:, NR1:NR1 + 96], stats[:, NRS:NRS + 96], stats[:, NSC:NSC + 96], ALU.mult))
        v2 = vop(lambda eng: eng.tensor_tensor(stats[:, NR1:NR1 + 96], stats[:, NR1:NR1 + 96], stats[:, NSC:NSC + 96], ALU.mult))
        wt("scalar", "v", v2)
        s2 = sop(lambda eng: eng.activation(stats[:, NR1:NR1 + 96], stats[:, NR1:NR1 + 96], AF.Sqrt, bias=EPS))
        wt("vector", "s", s2)
        vop(lambda eng: eng.reciprocal(stats[:, NR1:NR1 + 96], stats[:, NR1:NR1 + 96]))
        v_stats = vop(lambda eng: eng.tensor_copy(stats[:, NRS:NRS + 96], stats[:, NSC:NSC + 96]))
        xnbuf = [arB[:, 0:2048], arB[:, 2048:4096]]
        lnbuf = [arB[:, 4096:6144], arB[:, 6144:8192]]
        lastv = {0: 0, 1: 0}
        lastwr = {0: 0, 1: 0}
        last_e = 0
        for j, (src, i, col, tag) in enumerate(tile_jobs):
            bi = j % 2
            wt("sync", "v", lastv[bi])
            ld = dma("sync", xbuf[bi], src[i * 128:(i + 1) * 128, :])
            wt("vector", "sync", ld)
            wt("vector", "v", v_stats)
            wt("vector", "sync", lastwr[bi])
            v_xn = vop(lambda eng, bi=bi, col=col: eng.tensor_scalar(
                xnbuf[bi], xbuf[bi], stats[:, NMU + col:NMU + col + 1],
                stats[:, NRS + col:NRS + col + 1], ALU.add, ALU.mult))
            if tag == "bb":
                wt("sync", "v", v_xn)
                lastwr[bi] = dma("sync", xn_bb[i * 128:(i + 1) * 128, :], xnbuf[bi])
                lastv[bi] = v_xn
            else:
                wt("vector", "sync", last_e)
                v_f = vop(lambda eng, bi=bi, col=col: eng.tensor_scalar(
                    arE[:, 0:2048], xbuf[bi], stats[:, NMU + col:NMU + col + 1],
                    stats[:, NRS + col:NRS + col + 1], ALU.add, ALU.mult))
                lastv[bi] = v_f
                if tag == "own":
                    wt("sync", "v", v_f)
                    last_e = dma("sync", xn_own_d[i * 128:(i + 1) * 128, :], arE[:, 0:2048])
                v_l1 = vop(lambda eng, col=col: eng.scalar_tensor_tensor(
                    arE[:, 2048:4096], arE[:, 0:2048], stats[:, NR1 + col:NR1 + col + 1],
                    gbg[:], ALU.mult, ALU.mult))
                v_l2 = vop(lambda eng, bi=bi: eng.tensor_tensor(
                    lnbuf[bi], arE[:, 2048:4096], gbb[:], ALU.add))
                wt("sync", "v", v_l2)
                if tag == "own":
                    lastwr[bi] = dma("sync", ln1_own_d[i * 128:(i + 1) * 128, :], lnbuf[bi])
                else:
                    lastwr[bi] = dma("sync", ln1_ext_d[i * 128:(i + 1) * 128, :], lnbuf[bi])
        barrier()
        if debug:
            for i in range(16):
                ld = dma("sync", arD[:, 0:2048], xn_own_d[i * 128:(i + 1) * 128, :])
                wt("sync", "sync", ld)
                dma("sync", dbg["dbg_xn_own"][i * 128:(i + 1) * 128, :], arD[:, 0:2048])
            barrier()

        if debug:
            vt2 = vop(lambda eng: eng.tensor_copy(arD[:, 0:1024], wdtinT_sb[:]))
            wt("sync", "v", vt2)
            dma("sync", dbg["dbg_wdi_t2"], arD[:, 0:1024])
            barrier()
        tprev = {0: 0, 1: 0}
        for hc in range(16):
            bi = hc % 2
            wt("sync", "sync", tprev[bi])
            t1 = dma_t("sync", arB[:, bi * S:(bi + 1) * S], xn_bb[:, hc * 128:(hc + 1) * 128])
            wt("sync", "sync", t1)
            tprev[bi] = dma("sync", xnT_bb[hc * 128:(hc + 1) * 128, :], arB[:, bi * S:(bi + 1) * S])
        barrier()

        bct_stage_wr = 0
        for tc in range(8):
            lds = []
            for hc in range(16):
                lds.append(dma("sync", arB[:, hc * 512:(hc + 1) * 512],
                               xnT_bb[hc * 128:(hc + 1) * 128, tc * 512:(tc + 1) * 512]))
            wt("tensor", "sync", lds[-1])
            mms = [(lambda hc=hc: wdtinT_sb[:, hc * DTR:(hc + 1) * DTR],
                    lambda hc=hc: arB[:, hc * 512:(hc + 1) * 512]) for hc in range(16)]
            def ev_dt(eng, ps, tc=tc):
                return eng.copy(dtpre_bf[0:DTR, tc * 512:(tc + 1) * 512], ps[0:DTR, :])
            last_dtpre_ev, _ = mm_group(0, mms, ("s", ev_dt), m=DTR)
            mmsb = [(lambda hc=hc: wbcT_sb[:, hc * 32:(hc + 1) * 32],
                     lambda hc=hc: arB[:, hc * 512:(hc + 1) * 512]) for hc in range(16)]
            wt("scalar", "sync", bct_stage_wr)
            def ev_bc(eng, ps, tc=tc):
                return eng.copy(arF[0:32, 1536:2048], ps[0:32, :])
            etok, _ = mm_group(1, mmsb, ("s", ev_bc), m=32)
            wt("sync", "s", etok)
            bct_stage_wr = dma("sync", bct_d[0:32, tc * 512:(tc + 1) * 512], arF[0:32, 1536:2048])
        barrier()
        if debug:
            v9 = vop(lambda eng: eng.tensor_copy(arD[0:DTR, 0:4096].bitcast(BF16)[:, 0:4096], dtpre_bf[:]))
            v9b = vop(lambda eng: eng.tensor_copy(arE[0:DTR, 0:4096], arD[0:DTR, 0:4096].bitcast(BF16)[:, 0:4096]))
            wt("sync", "v", v9b)
            dma("sync", dbg["dbg_dtpre"], arE[0:DTR, 0:4096])
            v9c = vop(lambda eng: eng.tensor_copy(arD[:, 0:1024], wdtinT_sb[:]))
            wt("sync", "v", v9c)
            dma("sync", dbg["dbg_wdi"], arD[:, 0:1024])
            barrier()
        if debug:
            ldb = dma("sync", arF[0:32, 0:4096], bct_d[:, :])
            wt("vector", "sync", ldb)
            v1 = vop(lambda eng: eng.tensor_copy(arE[0:32, 0:S], arF[0:32, 0:4096]))
            wt("sync", "v", v1)
            dma("sync", dbg["dbg_bct"], arE[0:32, 0:S])
            barrier()

        sp1 = arE[:, 0:512]
        sp2 = arE[:, 512:1024]
        dtf = arE[:, 1024:1536]
        dterm = arE[:, 1536:2048]
        u_t = arF[:, 0:512]
        dt_bf_t = arF[:, 512:1024]
        du_t = arF[:, 1024:1536]
        prev_wr = [0, 0, 0]
        prev_u = 0
        prev_vdt = 0
        wt("tensor", "s", last_dtpre_ev)
        for hc in range(8):
            for tc in range(8):
                ch, n = bank_tok[2]
                wt("tensor", ch, n)
                tk = top(lambda eng, hc=hc, tc=tc: eng.matmul(
                    pb[2][:], wdtoutT_sb[0:DTR, hc * 128:(hc + 1) * 128],
                    dtpre_bf[0:DTR, tc * 512:(tc + 1) * 512], start=True, stop=True))
                wt("scalar", "v", prev_vdt)
                wt("scalar", "t", tk)
                sop(lambda eng, hc=hc: eng.activation(sp1, pb[2][:], AF.Relu, bias=bdt_sb[:, hc:hc + 1]))
                s_abs = sop(lambda eng, hc=hc: eng.activation(sp2, pb[2][:], AF.Abs, bias=bdt_sb[:, hc:hc + 1]))
                bank_tok[2] = ("s", s_abs)
                sop(lambda eng: eng.activation(sp2, sp2, AF.Exp, scale=-1.0))
                s_sp = sop(lambda eng: eng.activation(sp2, sp2, AF.Ln, bias=1.0))
                wt("vector", "s", s_sp)
                v_dt = vop(lambda eng: eng.tensor_tensor(dtf, sp1, sp2, ALU.add))
                prev_vdt = v_dt
                wt("vector", "sync", prev_wr[0])
                v_dtb = vop(lambda eng: eng.tensor_copy(dt_bf_t, dtf))
                wt("sync", "v", v_dtb)
                prev_wr[0] = dma("sync", dt_d[hc * 128:(hc + 1) * 128, tc * 512:(tc + 1) * 512], dt_bf_t)
                wt("sync", "v", prev_u)
                lu = dma("sync", u_t, xnT_bb[hc * 128:(hc + 1) * 128, tc * 512:(tc + 1) * 512])
                wt("vector", "sync", lu)
                wt("vector", "sync", prev_wr[1])
                v_du = vop(lambda eng: eng.scalar_tensor_tensor(du_t, u_t, 1.0, dt_bf_t, ALU.mult, ALU.mult))
                wt("sync", "v", v_du)
                prev_wr[1] = dma("sync", du_d[hc * 128:(hc + 1) * 128, tc * 512:(tc + 1) * 512], du_t)
                wt("vector", "sync", prev_wr[2])
                prev_u = vop(lambda eng, hc=hc: eng.tensor_scalar(
                    dterm, u_t, d_sb[:, hc:hc + 1], None, ALU.mult))
                wt("sync", "v", prev_u)
                prev_wr[2] = dma("sync", y_d[hc * 128:(hc + 1) * 128, tc * 512:(tc + 1) * 512], dterm)
        barrier()
        if debug:
            for hc in range(8):
                ld = dma("sync", arF[:, 0:4096], dt_d[hc * 128:(hc + 1) * 128, :])
                wt("vector", "sync", ld)
                v1 = vop(lambda eng: eng.tensor_copy(arD[:, 0:4096], arF[:, 0:4096]))
                wt("sync", "v", v1)
                dma("sync", dbg["dbg_dt"][hc * 128:(hc + 1) * 128, :], arD[:, 0:4096])
            barrier()

        for hc in range(16):
            dma_t("sync", arA[:, hc * TOK:(hc + 1) * TOK], ln1_own_d[:, hc * 128:(hc + 1) * 128])
        barrier()
        qstage = [arF[:, i * 512:(i + 1) * 512] for i in range(4)]
        qstage_wr = [0, 0, 0, 0]
        for half in range(2):
            lds = []
            for hc in range(16):
                lds.append(dma("sync", arB[:, hc * 1536:(hc + 1) * 1536],
                               qkvwT_d[hc * 128:(hc + 1) * 128, half * 1536:(half + 1) * 1536]))
            wt("tensor", "sync", lds[-1])
            for oc in range(12):
                for tcol in range(4):
                    mms = [(lambda hc=hc, oc=oc: arB[:, hc * 1536 + oc * 128: hc * 1536 + (oc + 1) * 128],
                            lambda hc=hc, tcol=tcol: arA[:, hc * TOK + tcol * 512: hc * TOK + (tcol + 1) * 512])
                           for hc in range(16)]
                    si = tcol
                    wt("scalar", "sync", qstage_wr[si])
                    def ev(eng, ps, si=si):
                        return eng.copy(qstage[si], ps[:])
                    etok, _ = mm_group(tcol % 2, mms, ("s", ev))
                    wt("sync", "s", etok)
                    qstage_wr[si] = dma("sync",
                                        qkvT_d[(half * 12 + oc) * 128:(half * 12 + oc + 1) * 128,
                                               tcol * 512:(tcol + 1) * 512], qstage[si])
        barrier()
        kv_cols = [(g * 768 + 512, g * 768 + 768) for g in range(NKV)]
        last_kv_mm = 0
        for slot in range(3):
            for hc in range(16):
                dma_t("sync", arB[:, hc * TOK:(hc + 1) * TOK],
                      ln1_ext_d[slot * TOK:(slot + 1) * TOK, hc * 128:(hc + 1) * 128])
            ln_t = cnt["sync"]
            wt("scalar", "t", last_kv_mm)
            lds = []
            for hc in range(16):
                for g in range(NKV):
                    c0, c1 = kv_cols[g]
                    lds.append(dma("scalar", arC[:, hc * 1024 + g * 256: hc * 1024 + (g + 1) * 256],
                                   qkvwT_d[hc * 128:(hc + 1) * 128, c0:c1]))
            wt("tensor", "sync", ln_t)
            wt("tensor", "sc", lds[-1])
            for oc in range(8):
                for tcol in range(4):
                    mms = [(lambda hc=hc, oc=oc: arC[:, hc * 1024 + oc * 128: hc * 1024 + (oc + 1) * 128],
                            lambda hc=hc, tcol=tcol: arB[:, hc * TOK + tcol * 512: hc * TOK + (tcol + 1) * 512])
                           for hc in range(16)]
                    si = tcol
                    wt("scalar", "sync", qstage_wr[si])
                    def ev(eng, ps, si=si):
                        return eng.copy(qstage[si], ps[:])
                    etok, mmtok = mm_group(tcol % 2, mms, ("s", ev))
                    last_kv_mm = mmtok
                    wt("sync", "s", etok)
                    qstage_wr[si] = dma("sync",
                                        kvT_d[(slot * 8 + oc) * 128:(slot * 8 + oc + 1) * 128,
                                              tcol * 512:(tcol + 1) * 512], qstage[si])
        barrier()

        if debug:
            for i in range(16):
                ld = dma("sync", arF[:, 0:2048], ln1_own_d[i * 128:(i + 1) * 128, :])
                wt("vector", "sync", ld)
                v1 = vop(lambda eng: eng.tensor_copy(arD[:, 0:2048], arF[:, 0:2048]))
                wt("sync", "v", v1)
                dma("sync", dbg["dbg_ln1"][i * 128:(i + 1) * 128, :], arD[:, 0:2048])
            for i in range(24):
                ld = dma("sync", arF[:, 0:2048], qkvT_d[i * 128:(i + 1) * 128, :])
                wt("vector", "sync", ld)
                v1 = vop(lambda eng: eng.tensor_copy(arD[:, 0:2048], arF[:, 0:2048]))
                wt("sync", "v", v1)
                dma("sync", dbg["dbg_qkvT"][i * 128:(i + 1) * 128, :], arD[:, 0:2048])
            for i in range(24):
                ld = dma("sync", arF[:, 0:2048], kvT_d[i * 128:(i + 1) * 128, :])
                wt("vector", "sync", ld)
                v1 = vop(lambda eng: eng.tensor_copy(arD[:, 0:2048], arF[:, 0:2048]))
                wt("sync", "v", v1)
                dma("sync", dbg["dbg_kvT"][i * 128:(i + 1) * 128, :], arD[:, 0:2048])
            barrier()

        prev_attn_wr = 0
        prev_vp = 0
        for chunk in range(16):
            wt("sync", "v", prev_vp)
            wt("sync", "sync", prev_attn_wr)
            l1 = dma_t("sync", sqa[:], qkvT_d[:, chunk * 128:(chunk + 1) * 128])
            l2 = dma_t("sync", sqb[:], kvT_d[:, chunk * 128:(chunk + 1) * 128])
            wt("vector", "sync", l2)
            for h in range(NH):
                g, m = h // QPK, h % QPK
                qof = g * 768 + m * 128
                kof_own = g * 768 + 512
                for j in range(4):
                    if j < 3:
                        ksl = sqb[:, j * 1024 + g * 256: j * 1024 + g * 256 + 128]
                    else:
                        ksl = sqa[:, kof_own:kof_own + 128]
                    vop(lambda eng, qof=qof, ksl=ksl, h=h, j=j:
                        eng.scalar_tensor_tensor(satt[:], sqa[:, qof:qof + 128], ATT_SCALE,
                                                 ksl, ALU.mult, ALU.mult,
                                                 accum_out=ssc[:, j * 16 + h: j * 16 + h + 1]))
            vop(lambda eng: eng.tensor_tensor(ssc[:], ssc[:], mv_sb[:], ALU.mult))
            vop(lambda eng: eng.tensor_tensor(ssc[:], ssc[:], ma_sb[:], ALU.add))
            vop(lambda eng: eng.tensor_tensor(ssm[:, 0:16], ssc[:, 0:16], ssc[:, 16:32], ALU.max))
            vop(lambda eng: eng.tensor_tensor(ssm[:, 16:32], ssc[:, 32:48], ssc[:, 48:64], ALU.max))
            vop(lambda eng: eng.tensor_tensor(ssm[:, 0:16], ssm[:, 0:16], ssm[:, 16:32], ALU.max))
            for j in range(4):
                vop(lambda eng, j=j: eng.tensor_tensor(
                    ssc[:, j * 16:(j + 1) * 16], ssc[:, j * 16:(j + 1) * 16], ssm[:, 0:16], ALU.subtract))
            v_pre_e = cnt["v"]
            wt("scalar", "v", v_pre_e)
            s_e = sop(lambda eng: eng.activation(ssc[:], ssc[:], AF.Exp))
            wt("vector", "s", s_e)
            vop(lambda eng: eng.tensor_tensor(ssm[:, 16:32], ssc[:, 0:16], ssc[:, 16:32], ALU.add))
            vop(lambda eng: eng.tensor_tensor(ssm[:, 32:48], ssc[:, 32:48], ssc[:, 48:64], ALU.add))
            vop(lambda eng: eng.tensor_tensor(ssm[:, 16:32], ssm[:, 16:32], ssm[:, 32:48], ALU.add))
            vop(lambda eng: eng.reciprocal(ssm[:, 16:32], ssm[:, 16:32]))
            for j in range(4):
                vop(lambda eng, j=j: eng.tensor_tensor(
                    ssc[:, j * 16:(j + 1) * 16], ssc[:, j * 16:(j + 1) * 16], ssm[:, 16:32], ALU.mult))
            prev_vp = cnt["v"]
            if debug and chunk == 0:
                wt("sync", "v", prev_vp)
                dma("sync", dbg["dbg_probs"], ssc[:])
            for h in range(NH):
                g = h // QPK
                vof_own = g * 768 + 640
                for j in range(4):
                    if j < 3:
                        vsl = sqb[:, j * 1024 + g * 256 + 128: j * 1024 + g * 256 + 256]
                    else:
                        vsl = sqa[:, vof_own:vof_own + 128]
                    if j == 0:
                        vop(lambda eng, h=h, vsl=vsl: eng.tensor_scalar(
                            sso[:, h * 128:(h + 1) * 128], vsl, ssc[:, h:h + 1], None, ALU.mult))
                    else:
                        vop(lambda eng, h=h, j=j, vsl=vsl: eng.scalar_tensor_tensor(
                            sso[:, h * 128:(h + 1) * 128], vsl, ssc[:, j * 16 + h:j * 16 + h + 1],
                            sso[:, h * 128:(h + 1) * 128], ALU.mult, ALU.add))
            prev_vp = cnt["v"]
            wt("sync", "v", prev_vp)
            prev_attn_wr = dma("sync", attn_d[chunk * 128:(chunk + 1) * 128, :], sso[:])
        barrier()
        if debug:
            for i in range(16):
                ld = dma("sync", arF[:, 0:2048], attn_d[i * 128:(i + 1) * 128, :])
                wt("vector", "sync", ld)
                v1 = vop(lambda eng: eng.tensor_copy(arD[:, 0:2048], arF[:, 0:2048]))
                wt("sync", "v", v1)
                dma("sync", dbg["dbg_attn"][i * 128:(i + 1) * 128, :], arD[:, 0:2048])
            barrier()

        for hc in range(16):
            dma_t("sync", arA[:, hc * TOK:(hc + 1) * TOK], attn_d[:, hc * 128:(hc + 1) * 128])
        barrier()
        hstage = [arE[:, 2048 + i * 512:2048 + (i + 1) * 512] for i in range(2)]
        hstage_wr = [0, 0]
        last_proj_mm = 0
        for hs in range(4):
            wt("scalar", "t", last_proj_mm)
            lds = []
            for hc in range(16):
                lds.append(dma("scalar", arC[:, hc * 512:(hc + 1) * 512],
                               projwT_d[hc * 128:(hc + 1) * 128, hs * 512:(hs + 1) * 512]))
            wt("tensor", "sc", lds[-1])
            for t128 in range(16):
                si = t128 % 2
                wt("sync", "sync", hstage_wr[si])
                lx = dma("sync", hstage[si], xn_own_d[t128 * 128:(t128 + 1) * 128, hs * 512:(hs + 1) * 512])
                wt("vector", "sync", lx)
                mms = [(lambda hc=hc, t128=t128: arA[:, hc * TOK + t128 * 128: hc * TOK + (t128 + 1) * 128],
                        lambda hc=hc: arC[:, hc * 512:(hc + 1) * 512]) for hc in range(16)]
                def ev(eng, ps, si=si):
                    return eng.scalar_tensor_tensor(hstage[si], ps[:], 1.0, hstage[si], ALU.mult, ALU.add)
                etok, mmtok = mm_group(t128 % 2, mms, ("v", ev))
                last_proj_mm = mmtok
                wt("sync", "v", etok)
                wr = dma("sync", haa_d[t128 * 128:(t128 + 1) * 128, hs * 512:(hs + 1) * 512], hstage[si])
                hstage_wr[si] = wr
        barrier()

        emit_gamma(ln2_w, ln2_b)
        barrier()
        prev_x_wr = 0
        for i in range(16):
            ld = dma("sync", arD[:, 0:2048], haa_d[i * 128:(i + 1) * 128, :])
            wt("vector", "sync", ld)
            for q in range(4):
                vop(lambda eng, q=q: eng.bn_stats(bn6[:, q * 6:(q + 1) * 6], arD[:, q * 512:(q + 1) * 512]))
            v2 = vop(lambda eng: eng.bn_aggr(bn2[:], bn6[:]))
            wt("scalar", "v", v2)
            s_q = sop(lambda eng: eng.activation(bn2[:, 1:2], bn2[:, 1:2], AF.Sqrt, bias=EPS))
            wt("vector", "s", s_q)
            vop(lambda eng: eng.reciprocal(bn2[:, 1:2], bn2[:, 1:2]))
            vop(lambda eng: eng.tensor_scalar(bn2[:, 0:1], bn2[:, 0:1], -1.0, None, ALU.mult))
            vop(lambda eng: eng.tensor_scalar(arE[:, 0:2048], arD[:, 0:2048], bn2[:, 0:1], bn2[:, 1:2], ALU.add, ALU.mult))
            vop(lambda eng: eng.tensor_tensor(arE[:, 0:2048], arE[:, 0:2048], gbg[:], ALU.mult))
            vop(lambda eng: eng.tensor_tensor(arE[:, 0:2048], arE[:, 0:2048], gbb[:], ALU.add))
            wt("vector", "sync", prev_x_wr)
            v_c = vop(lambda eng: eng.tensor_copy(arF[:, 0:2048], arE[:, 0:2048]))
            wt("sync", "v", v_c)
            prev_x_wr = dma("sync", xln2_d[i * 128:(i + 1) * 128, :], arF[:, 0:2048])
        barrier()
        for hc in range(16):
            dma_t("sync", arA[:, hc * TOK:(hc + 1) * TOK], xln2_d[:, hc * 128:(hc + 1) * 128])
        barrier()
        for t128 in range(16):
            mms = [(lambda hc=hc, t128=t128: arA[:, hc * TOK + t128 * 128: hc * TOK + (t128 + 1) * 128],
                    lambda hc=hc: routw_sb[:, hc * NE:(hc + 1) * NE]) for hc in range(16)]
            def ev(eng, ps):
                return eng.tensor_copy(ssm[:, 0:8], ps[:, 0:8])
            mm_group(4, mms, ("v", ev), nn=NE)
            vop(lambda eng: eng.tensor_reduce(ssm[:, 8:9], ssm[:, 0:8], mybir.AxisListType.X, ALU.max, negate=True))
            v3 = vop(lambda eng: eng.tensor_scalar(ssm[:, 0:8], ssm[:, 0:8], ssm[:, 8:9], None, ALU.add))
            wt("scalar", "v", v3)
            s_e = sop(lambda eng: eng.activation(ssm[:, 0:8], ssm[:, 0:8], AF.Exp))
            wt("vector", "s", s_e)
            vop(lambda eng: eng.tensor_reduce(ssm[:, 8:9], ssm[:, 0:8], mybir.AxisListType.X, ALU.add))
            vop(lambda eng: eng.reciprocal(ssm[:, 8:9], ssm[:, 8:9]))
            vop(lambda eng: eng.tensor_scalar(ssm[:, 0:8], ssm[:, 0:8], ssm[:, 8:9], None, ALU.mult))
            vop(lambda eng: eng.max(ssm[:, 16:24], ssm[:, 0:8]))
            vop(lambda eng: eng.tensor_tensor(ssm[:, 24:25], ssm[:, 16:17], ssm[:, 17:18], ALU.add))
            vop(lambda eng: eng.reciprocal(ssm[:, 24:25], ssm[:, 24:25]))
            vop(lambda eng: eng.tensor_scalar(ssm[:, 32:40], ssm[:, 0:8], ssm[:, 17:18], None, ALU.is_ge))
            vop(lambda eng: eng.tensor_tensor(ssm[:, 32:40], ssm[:, 32:40], ssm[:, 0:8], ALU.mult))
            vop(lambda eng, t128=t128: eng.tensor_scalar(
                gates_sb[:, t128 * NE:(t128 + 1) * NE], ssm[:, 32:40], ssm[:, 24:25], None, ALU.mult))
        barrier()
        if debug:
            v1 = vop(lambda eng: eng.tensor_copy(arD[:, 0:128], gates_sb[:]))
            wt("sync", "v", v1)
            dma("sync", dbg["dbg_gates"], arD[:, 0:128])
            for i in range(16):
                ld = dma("sync", arF[:, 0:2048], xln2_d[i * 128:(i + 1) * 128, :])
                wt("vector", "sync", ld)
                v1 = vop(lambda eng: eng.tensor_copy(arD[:, 0:2048], arF[:, 0:2048]))
                wt("sync", "v", v1)
                dma("sync", dbg["dbg_xln2"][i * 128:(i + 1) * 128, :], arD[:, 0:2048])
            for i in range(16):
                ld = dma("sync", arD[:, 0:2048], haa_d[i * 128:(i + 1) * 128, :])
                wt("sync", "sync", ld)
                dma("sync", dbg["dbg_haa"][i * 128:(i + 1) * 128, :], arD[:, 0:2048])
            barrier()

        dma("sync", sel_re, sel_bc)
        n_bct = dma("sync", bct_d2 := bct_re, bct_d[:, :])
        wt("tensor", "sync", n_bct)
        v_z = vop(lambda eng: eng.memset(mst[:], 0.0))
        prev_cx_dma = {}
        prev_scan_of_abuf = {}
        last_cx_v = 0
        last_sa_s = 0
        for tcg in range(NTG):
            wt("sync", "v", last_cx_v)
            wt("sync", "s", last_sa_s)
            lds = []
            for hc in range(8):
                lds.append(dma("sync", mdt[:, hc * TG:(hc + 1) * TG],
                               dt_d[hc * 128:(hc + 1) * 128, tcg * TG:(tcg + 1) * TG]))
                lds.append(dma("sync", mdu[:, hc * TG:(hc + 1) * TG],
                               du_d[hc * 128:(hc + 1) * 128, tcg * TG:(tcg + 1) * TG]))
            dtld = lds[-1]
            wt("scalar", "sync", dtld)
            wt("vector", "sync", dtld)
            for n in range(NST):
                wt("scalar", "v", last_cx_v)
                for half in range(2):
                    ch, nn = bank_tok[6]
                    wt("tensor", ch, nn)
                    tkB = top(lambda eng, n=n, half=half, tcg=tcg: eng.matmul(
                        pb[6][:], sel_re[:, n * 128:(n + 1) * 128],
                        bct_re[:, tcg * TG + half * 512: tcg * TG + (half + 1) * 512],
                        start=True, stop=True))
                    wt("scalar", "t", tkB)
                    bank_tok[6] = ("s", sop(lambda eng, half=half: eng.copy(
                        mbc[:, half * 512:(half + 1) * 512], pb[6][:])))
                    ch, nn = bank_tok[5]
                    wt("tensor", ch, nn)
                    tkC = top(lambda eng, n=n, half=half, tcg=tcg: eng.matmul(
                        pb[5][:], sel_re[:, (16 + n) * 128:(17 + n) * 128],
                        bct_re[:, tcg * TG + half * 512: tcg * TG + (half + 1) * 512],
                        start=True, stop=True))
                    wt("scalar", "t", tkC)
                    bank_tok[5] = ("s", sop(lambda eng, half=half: eng.copy(
                        mbc[:, TG + half * 512: TG + (half + 1) * 512], pb[5][:])))
                bc_tok = cnt["s"]
                for hc in range(8):
                    it = n * 8 + hc
                    bi = it % 2
                    wt("scalar", "v", prev_scan_of_abuf.get(bi, 0))
                    s_a = sop(lambda eng, hc=hc, n=n, bi=bi: eng.activation(
                        ma_[:, bi * TG:(bi + 1) * TG], mdt[:, hc * TG:(hc + 1) * TG],
                        AF.Exp, scale=a_sb[:, hc * NST + n: hc * NST + n + 1]))
                    last_sa_s = s_a
                    wt("vector", "s", bc_tok)
                    wt("vector", "gp", prev_cx_dma.get(bi, 0))
                    v_bu = vop(lambda eng, hc=hc, bi=bi: eng.tensor_tensor(
                        mbu[:, bi * TG:(bi + 1) * TG], mdu[:, hc * TG:(hc + 1) * TG],
                        mbc[:, 0:TG], ALU.mult))
                    wt("vector", "s", s_a)
                    v_sc = vop(lambda eng, hc=hc, n=n, bi=bi: eng.tensor_tensor_scan(
                        mx[:, bi * TG:(bi + 1) * TG], ma_[:, bi * TG:(bi + 1) * TG],
                        mbu[:, bi * TG:(bi + 1) * TG],
                        mst[:, hc * 16 + n: hc * 16 + n + 1], ALU.mult, ALU.add))
                    prev_scan_of_abuf[bi] = v_sc
                    vop(lambda eng, hc=hc, n=n, bi=bi: eng.tensor_copy(
                        mst[:, hc * 16 + n: hc * 16 + n + 1], mx[:, (bi + 1) * TG - 1:(bi + 1) * TG]))
                    v_cx = vop(lambda eng, bi=bi: eng.tensor_tensor(
                        mcx[:, bi * TG:(bi + 1) * TG], mx[:, bi * TG:(bi + 1) * TG],
                        mbc[:, TG:2 * TG], ALU.mult))
                    last_cx_v = v_cx
                    wt("gpsimd", "v", v_cx)
                    prev_cx_dma[bi] = dma("gpsimd", y_d[hc * 128:(hc + 1) * 128, tcg * TG:(tcg + 1) * TG],
                                          mcx[:, bi * TG:(bi + 1) * TG], accum_op=ALU.add)
                wt("gpsimd", "gp", cnt["gp"])
        barrier()
        prev_v = 0
        prev_w = 0
        for hc in range(8):
            for tc in range(4):
                wt("sync", "v", prev_v)
                ld = dma("sync", arD[:, 0:1024], y_d[hc * 128:(hc + 1) * 128, tc * 1024:(tc + 1) * 1024])
                wt("vector", "sync", ld)
                wt("vector", "sync", prev_w)
                prev_v = vop(lambda eng: eng.tensor_copy(arF[:, 0:1024], arD[:, 0:1024]))
                wt("sync", "v", prev_v)
                prev_w = dma("sync", y_bf_d[hc * 128:(hc + 1) * 128, tc * 1024:(tc + 1) * 1024], arF[:, 0:1024])
        barrier()
        tprev = {0: 0, 1: 0}
        for tc in range(32):
            bi = tc % 2
            wt("sync", "sync", tprev[bi])
            t1 = dma_t("sync", arF[:, bi * 1024:(bi + 1) * 1024], y_bf_d[:, tc * 128:(tc + 1) * 128])
            wt("sync", "sync", t1)
            tprev[bi] = dma("sync", y_tok_d[tc * 128:(tc + 1) * 128, :], arF[:, bi * 1024:(bi + 1) * 1024])
        barrier()
        if debug:
            for i in range(32):
                ld = dma("sync", arF[:, 0:1024], y_tok_d[i * 128:(i + 1) * 128, :])
                wt("vector", "sync", ld)
                v1 = vop(lambda eng: eng.tensor_copy(arD[:, 0:1024], arF[:, 0:1024]))
                wt("sync", "v", v1)
                dma("sync", dbg["dbg_ytok"][i * 128:(i + 1) * 128, :], arD[:, 0:1024])
            barrier()

        gp_base = cnt["gp"]
        def fn_xchg(eng, gp_base=gp_base):
            rk = eng.value_load(rank_in)
            k = rk % 2
            for body in eng.Switch(k, 2):
                other = 1 - body
                eng.dma_start(out=xchg_y[body, :, :],
                              in_=y_tok_d[other * TOK:(other + 1) * TOK, :]).then_inc(sm_gp, 16)
                eng.dma_start(out=y_cat_d[:, body * HH:(body + 1) * HH],
                              in_=y_tok_d[body * TOK:(body + 1) * TOK, :]).then_inc(sm_gp, 16)
                eng.wait_ge(sm_gp, 16 * (gp_base + 2))
                eng.store(xchg_fl[body:body + 1, 0:1], MAGIC)
                with eng.register(f"xf{body}") as f:
                    eng.reg_mov(f, 1)
                    with eng.While(f):
                        eng.load(f, xchg_fl[other:other + 1, 0:1])
                        eng.reg_alu(f, f, MAGIC, ALU.bitwise_xor)
                eng.dma_start(out=y_cat_d[:, other * HH:(other + 1) * HH],
                              in_=xchg_y[other, :, :]).then_inc(sm_gp, 16)
            eng.wait_ge(sm_gp, 16 * (gp_base + 3))
            eng.nop().then_inc(sm_p, 1)
        on("gpsimd", fn_xchg)
        cnt["gp"] += 3
        cnt["p"] += 1
        xchg_done_p = cnt["p"]

        blocks = []
        for e in range(NE):
            for fhalf in range(2):
                for sub in range(4):
                    blocks.append(("h1", e, fhalf, sub))
                for hs in range(4):
                    blocks.append(("w2", e, fhalf, hs))
        NBLK = len(blocks)
        ld_tok = {}
        mm_tok = {}
        last_w2_mm = 0
        g_done_of = {}
        moe_first_acc = True

        def emit_loads(i):
            kind, e, fhalf, q = blocks[i]
            half = arC[:, (i % 2) * 8192:(i % 2) * 8192 + 8192]
            if i - 2 >= 0:
                wt("gpsimd", "t", mm_tok[i - 2])
            toks = []
            if kind == "h1":
                f0 = fhalf * 2048 + q * 512
                for hc in range(16):
                    toks.append(dma("gpsimd", half[:, hc * 512:(hc + 1) * 512],
                                    moe_w1[e, hc * 128:(hc + 1) * 128, f0:f0 + 512]))
            else:
                for fc in range(16):
                    toks.append(dma("gpsimd", half[:, fc * 512:(fc + 1) * 512],
                                    moe_w2[e, fhalf * 2048 + fc * 128: fhalf * 2048 + (fc + 1) * 128,
                                           q * 512:(q + 1) * 512]))
            ld_tok[i] = toks[-1]

        w2_state = {"last_mm": 0, "round": 0}

        def emit_mms(i):
            kind, e, fhalf, q = blocks[i]
            half = arC[:, (i % 2) * 8192:(i % 2) * 8192 + 8192]
            wt("tensor", "gp", ld_tok[i])
            if kind == "h1":
                if q == 0:
                    wt("scalar", "t", w2_state["last_mm"])
                last = 0
                for fc4 in range(4):
                    for tcol in range(4):
                        mms = [(lambda hc=hc, fc4=fc4: half[:, hc * 512 + fc4 * 128: hc * 512 + (fc4 + 1) * 128],
                                lambda hc=hc, tcol=tcol: arA[:, hc * TOK + tcol * 512: hc * TOK + (tcol + 1) * 512])
                               for hc in range(16)]
                        grow_ = q * 4 + fc4
                        def ev(eng, ps, grow_=grow_, tcol=tcol):
                            return eng.activation(
                                arB[:, grow_ * TOK + tcol * 512: grow_ * TOK + (tcol + 1) * 512],
                                ps[:], AF.Gelu)
                        etok, mmtok = mm_group((fc4 * 4 + tcol) % 2, mms, ("s", ev))
                        last = mmtok
                mm_tok[i] = last
                if q == 3:
                    g_done_of[(e, fhalf)] = cnt["s"]
            else:
                wt("tensor", "s", g_done_of[(e, fhalf)])
                wt("gpsimd", "gp", w2_state["round"])
                last = 0
                for t128 in range(16):
                    si = t128 % 4
                    mms = [(lambda fc=fc, t128=t128: arB[:, fc * TOK + t128 * 128: fc * TOK + (t128 + 1) * 128],
                            lambda fc=fc: half[:, fc * 512:(fc + 1) * 512]) for fc in range(16)]
                    wt("scalar", "gp", prev_estage_gp[si])
                    def ev(eng, ps, e=e, t128=t128, si=si):
                        return eng.activation(estage[si], ps[:], AF.Copy,
                                              scale=gates_sb[:, t128 * NE + e: t128 * NE + e + 1])
                    etok, mmtok = mm_group(2 + (t128 % 2), mms, ("s", ev))
                    last = mmtok
                    wt("gpsimd", "s", etok)
                    if e == 0 and fhalf == 0:
                        prev_estage_gp[si] = dma("gpsimd",
                                                 moeacc[t128 * 128:(t128 + 1) * 128, q * 512:(q + 1) * 512],
                                                 estage[si])
                    else:
                        prev_estage_gp[si] = dma("gpsimd",
                                                 moeacc[t128 * 128:(t128 + 1) * 128, q * 512:(q + 1) * 512],
                                                 estage[si], accum_op=ALU.add)
                mm_tok[i] = last
                w2_state["last_mm"] = last
                w2_state["round"] = cnt["gp"]

        prev_estage_gp = [0, 0, 0, 0]
        estage = [arE[:, i * 512:(i + 1) * 512] for i in range(4)]
        emit_loads(0)
        emit_loads(1)
        for i in range(NBLK):
            emit_mms(i)
            if i + 2 < NBLK:
                emit_loads(i + 2)
        barrier()
        if debug:
            for i in range(16):
                ld = dma("sync", arD[:, 0:2048], moeacc[i * 128:(i + 1) * 128, :])
                wt("sync", "sync", ld)
                dma("sync", dbg["dbg_moe"][i * 128:(i + 1) * 128, :], arD[:, 0:2048])
            barrier()

        wt("vector", "p", xchg_done_p)
        wt("sync", "p", xchg_done_p)
        prev_out = 0
        prev_y_v = 0
        for t128 in range(16):
            wt("sync", "v", prev_y_v)
            wt("sync", "sync", prev_out)
            l1 = dma("sync", arD[:, 0:2048], x_own[t128 * 128:(t128 + 1) * 128, :])
            l2 = dma("sync", arD[:, 2048:4096], haa_d[t128 * 128:(t128 + 1) * 128, :])
            l3 = dma("sync", arE[:, 0:2048], moeacc[t128 * 128:(t128 + 1) * 128, :])
            l4 = dma("sync", arF[:, 0:2048], y_cat_d[t128 * 128:(t128 + 1) * 128, :])
            wt("vector", "sync", l4)
            vop(lambda eng: eng.tensor_tensor(arD[:, 0:2048], arD[:, 0:2048], arD[:, 2048:4096], ALU.add))
            vop(lambda eng: eng.tensor_tensor(arD[:, 0:2048], arD[:, 0:2048], arE[:, 0:2048], ALU.add))
            prev_y_v = vop(lambda eng: eng.tensor_tensor(arD[:, 0:2048], arD[:, 0:2048], arF[:, 0:2048], ALU.add))
            wt("sync", "v", prev_y_v)
            prev_out = dma("sync", out_slab[t128 * 128:(t128 + 1) * 128, :], arD[:, 0:2048])
        wt("sync", "sync", cnt["sync"])
        wt("gpsimd", "gp", cnt["gp"])
        barrier()

        block = ctx.enter_context(nc.Block())

        @block.sync
        def _(eng):
            for fn in rec["sync"]:
                fn(eng)

        @block.gpsimd
        def _(eng):
            for fn in rec["gpsimd"]:
                fn(eng)

        @block.vector
        def _(eng):
            for fn in rec["vector"]:
                fn(eng)

        @block.scalar
        def _(eng):
            for fn in rec["scalar"]:
                fn(eng)

        @block.tensor
        def _(eng):
            for fn in rec["tensor"]:
                fn(eng)

    return nc



def _host_inputs(inputs, c):
    b, k = c // 2, c % 2
    x = np.asarray(inputs["L0_attn_input"], np.float32).reshape(B, S, H)
    perm = np.concatenate([np.arange(k * HH, (k + 1) * HH), np.arange((1 - k) * HH, (2 - k) * HH)])
    s0 = k * TOK
    slots = [bb for bb in range(B) if bb != b]
    mv = np.zeros((128, 64), np.float32)
    mam = np.zeros((128, 64), np.float32)
    for j in range(4):
        bj = slots[j] if j < 3 else b
        if bj <= b:
            mv[:, j * 16:(j + 1) * 16] = 1.0
        else:
            mam[:, j * 16:(j + 1) * 16] = NEG
    h0 = k * HH
    sel = np.zeros((32, 32 * 128), ml_dtypes.bfloat16)
    for i in range(32):
        sel[i, i * 128:(i + 1) * 128] = 1.0
    return {
        "x_own": np.ascontiguousarray(x[b, s0:s0 + TOK, :]),
        "x_bb": np.ascontiguousarray(x[b][:, perm]),
        "x_ext": np.ascontiguousarray(np.concatenate([x[bb, s0:s0 + TOK, :] for bb in slots], 0)),
        "wdtin_p": np.ascontiguousarray(np.asarray(inputs["L0_attn_W_dt_in"], np.float32)[:, perm]),
        "wbc_p": np.ascontiguousarray(np.concatenate(
            [np.asarray(inputs["L0_attn_W_B"], np.float32),
             np.asarray(inputs["L0_attn_W_C"], np.float32)], 0)[:, perm]),
        "wdtout_h": np.ascontiguousarray(np.asarray(inputs["L0_attn_W_dt_out"], np.float32)[h0:h0 + HH, :]),
        "bdt_h": np.ascontiguousarray(np.asarray(inputs["L0_attn_b_dt_out"], np.float32)[h0:h0 + HH]),
        "alog_h": np.ascontiguousarray(np.asarray(inputs["L0_attn_A_log"], np.float32)[h0:h0 + HH, :]),
        "d_h": np.ascontiguousarray(np.asarray(inputs["L0_attn_D"], np.float32)[h0:h0 + HH]),
        "ln1_w": np.asarray(inputs["L0_ffn_ln1_weight"], np.float32),
        "ln1_b": np.asarray(inputs["L0_ffn_ln1_bias"], np.float32),
        "ln2_w": np.asarray(inputs["L0_ffn_ln2_weight"], np.float32),
        "ln2_b": np.asarray(inputs["L0_ffn_ln2_bias"], np.float32),
        "qkv_w": np.asarray(inputs["L0_ffn_qkv_weight"], np.float32),
        "proj_w": np.asarray(inputs["L0_ffn_proj_weight"], np.float32),
        "router_w": np.asarray(inputs["L0_ffn_router_weight"], np.float32),
        "moe_w1": np.asarray(inputs["L0_ffn_moe_w1"], np.float32),
        "moe_w2": np.asarray(inputs["L0_ffn_moe_w2"], np.float32),
        "rank": np.array([[c]], np.int32),
        "eye_bf": np.eye(128, dtype=ml_dtypes.bfloat16),
        "ones_row": np.ones((1, 128), ml_dtypes.bfloat16),
        "sel_bc": sel,
        "mv_t": mv,
        "ma_t": mam,
    }


_NC_CACHE = {}


def kernel(**inputs):
    key = ("v1", DEBUG)
    if key not in _NC_CACHE:
        _NC_CACHE[key] = build_nc(DEBUG)
    nc = _NC_CACHE[key]
    in_maps = [_host_inputs(inputs, c) for c in range(NCORES)]
    res = run_bass_kernel_spmd(nc, in_maps, core_ids=list(range(NCORES)))
    out = np.zeros((B, S, H), np.float32)
    for c in range(NCORES):
        b, k = c // 2, c % 2
        out[b, k * TOK:(k + 1) * TOK, :] = res.results[c]["out_slab"]
    return out
